# version 12
# baseline (speedup 1.0000x reference)
import contextlib
import numpy as np

import concourse.bass as bass
import concourse.mybir as mybir
from concourse.bass_utils import run_bass_kernel_spmd

F32 = mybir.dt.float32
BF16 = mybir.dt.bfloat16
I32 = mybir.dt.int32
AF = mybir.ActivationFunctionType
OP = mybir.AluOpType

P = 128
B, T, D, N, HID = 4, 4096, 1024, 256, 3072
TL = T // 2
ST = 512
NST = TL // ST
DC = D // P
NCC = N // P
HC = HID // P
SCALE = float(np.float32(1.0 / 127.0))
EPS = 1e-5


class Sched:

    def __init__(self):
        self.steps = []
        self.seq = {}
        self.last_w = {}
        self.readers = {}
        self.emitted_wait = {}
        self.dma_sems = set()
        self.pending = {}

    def step(self, sec, fn, reads=(), writes=(), extra_deps=(), sem=None,
             n=1, lazy=False):
        if sem is None:
            sem = sec
        pf = getattr(self, "pending_fence", None)
        if pf and sec in pf:
            extra_deps = list(extra_deps) + pf.pop(sec)
        if n == 16:
            self.dma_sems.add(sem)
        deps = {}

        def add(dep):
            if dep is None:
                return
            e, v = dep
            if v == "PENDING":
                assert e == sem, (
                    f"dependency on a lazy step's output via sem {e} from a "
                    f"step on sem {sem}; make that producer non-lazy")
                return
            if deps.get(e, 0) < v:
                deps[e] = v

        for t in reads:
            add(self.last_w.get(t))
        for t in writes:
            add(self.last_w.get(t))
            for r in self.readers.get(t, ()):
                add(r)
        for d in extra_deps:
            add(d)

        if sec == "tensor":
            deps.pop("tensor", None)

        for (e, v) in deps.items():
            if e in self.dma_sems:
                assert v == self.seq.get(e, 0), (
                    f"stale DMA lane wait: need {e}>={v}, lane at "
                    f"{self.seq.get(e, 0)}")

        waits = []
        for (e, v) in sorted(deps.items(), key=str):
            key = (sec, e)
            if self.emitted_wait.get(key, 0) >= v:
                continue
            self.emitted_wait[key] = v
            waits.append((e, v))

        if lazy:
            assert sec == "tensor", "lazy steps are tensor-only"
            tok = (sem, "PENDING")
            self.steps.append((sec, waits, fn, None, 0))
            pend = self.pending.setdefault(sem, [])
            for t in reads:
                pend.append(("r", t))
            for t in writes:
                self.last_w[t] = tok
                self.readers[t] = []
                pend.append(("w", t))
            return tok

        val_after = self.seq.get(sem, 0) + n
        self.seq[sem] = val_after
        self.steps.append((sec, waits, fn, sem, n))
        tok = (sem, val_after)
        for kind, t in self.pending.pop(sem, ()):
            if kind == "w":
                if self.last_w.get(t, (None, None))[1] == "PENDING":
                    self.last_w[t] = tok
            else:
                self.readers.setdefault(t, []).append(tok)
        for t in reads:
            self.readers.setdefault(t, []).append(tok)
        for t in writes:
            self.last_w[t] = tok
            self.readers[t] = []
        return tok

    def fence(self, sems_):
        return [(s, self.seq.get(s, 0)) for s in sems_]

    def barrier(self, exclude=()):
        f = self.fence([k for k in self.seq.keys() if k not in exclude])
        self.pending_fence = {s: list(f) for s in
                              ("tensor", "scalar", "vector", "gpsimd", "sync")}

    def emit(self, name, eng_obj, semmap):
        for (sec, waits, fn, sem, n) in self.steps:
            if sec != name:
                continue
            for (s, v) in waits:
                eng_obj.wait_ge(semmap[s], v)
            inst = fn(eng_obj)
            if sem is not None:
                inst.then_inc(semmap[sem], n)


def build_nc():
    nc = bass.Bass("TRN2", target_bir_lowering=False)

    u_main = nc.dram_tensor("u_main", [TL, D], F32, kind="ExternalInput").ap()
    u_pre = nc.dram_tensor("u_pre", [TL, D], F32, kind="ExternalInput").ap()
    wl_h = nc.dram_tensor("wl", [N, D], F32, kind="ExternalInput").ap()
    bl_h = nc.dram_tensor("bl", [N], F32, kind="ExternalInput").ap()
    wb_h = nc.dram_tensor("wb", [N, D], F32, kind="ExternalInput").ap()
    wc_h = nc.dram_tensor("wc", [D, N], F32, kind="ExternalInput").ap()
    dp_h = nc.dram_tensor("dp", [D], F32, kind="ExternalInput").ap()
    wg_h = nc.dram_tensor("wg", [HID, D], I32, kind="ExternalInput").ap()
    wu_h = nc.dram_tensor("wu", [HID, D], I32, kind="ExternalInput").ap()
    wd_h = nc.dram_tensor("wd", [D, HID], I32, kind="ExternalInput").ap()
    out_h = nc.dram_tensor("out", [TL, D], F32, kind="ExternalOutput").ap()

    u_s = nc.dram_tensor("u_s", [2, T, D], BF16).ap()
    x1_s = nc.dram_tensor("x1_s", [2, TL, D], BF16).ap()
    wgT_s = nc.dram_tensor("wgT_s", [HC, P, D], BF16).ap()
    wuT_s = nc.dram_tensor("wuT_s", [HC, P, D], BF16).ap()
    wdT_s = nc.dram_tensor("wdT_s", [HC, P, D], BF16).ap()

    sc = Sched()
    sem_names = ["tensor", "scalar", "vector", "gpsimd",
                 "qc0", "qc1", "qu0", "qu1", "qp0", "qp1", "qps0", "qps1",
                 "qw0", "qw1", "qw2", "qw3", "qx0", "qx1",
                 "qt0", "qt1", "qt2", "qt3", "qt4", "qt5", "qt6", "qt7",
                 "qxt0", "qxt1", "qxt2", "qxt3", "qus0", "qus1", "qxs0", "qxs1", "qul0", "qul1", "qxl0", "qxl1", "qpw0", "qpw1",
                 "qg0", "qg1", "qq0", "qq1", "qd0", "qd1", "qo0", "qo1"]

    with contextlib.ExitStack() as ctx:
        E = ctx.enter_context

        id32 = E(nc.sbuf_tensor("id32", [P, P], F32))
        idbf = E(nc.sbuf_tensor("idbf", [P, P], BF16))
        eps_sb = E(nc.sbuf_tensor("eps_sb", [P, 1], F32))
        bl_sb = E(nc.sbuf_tensor("bl_sb", [P, NCC], F32))
        dp1_sb = E(nc.sbuf_tensor("dp1_sb", [P, DC], F32))
        wlbT_hi = E(nc.sbuf_tensor("wlbT_hi", [P, 4 * D], BF16))
        wlbT_lo = E(nc.sbuf_tensor("wlbT_lo", [P, 4 * D], BF16))
        wcT_hi = E(nc.sbuf_tensor("wcT_hi", [P, NCC * D], BF16))
        wcT_lo = E(nc.sbuf_tensor("wcT_lo", [P, NCC * D], BF16))

        ps = [E(nc.psum_tensor(f"ps{i}", [P, 512], F32)) for i in range(4)]
        pt = [E(nc.psum_tensor(f"pt{i}", [P, 1024], F32)) for i in range(2)]

        def bank(i):
            if i < 4:
                return ps[i][:, 0:512]
            j, half = divmod(i - 4, 2)
            return pt[j][:, half * 512:(half + 1) * 512]

        sems = {s: E(nc.semaphore(s)) for s in sem_names}
        block = E(nc.Block())

        lnb = {}

        sc.step("gpsimd", lambda e: e.memset(id32[:, :], 0.0), writes=["id32"])
        sc.step("gpsimd", lambda e: e.affine_select(
            out=id32[:, :], in_=id32[:, :], compare_op=OP.not_equal, fill=1.0,
            base=0, pattern=[[-1, P]], channel_multiplier=1),
            reads=["id32"], writes=["id32"])
        sc.step("gpsimd", lambda e: e.memset(idbf[:, :], 0.0), writes=["idbf"])
        sc.step("gpsimd", lambda e: e.affine_select(
            out=idbf[:, :], in_=idbf[:, :], compare_op=OP.not_equal, fill=1.0,
            base=0, pattern=[[-1, P]], channel_multiplier=1),
            reads=["idbf"], writes=["idbf"])
        sc.step("gpsimd", lambda e: e.memset(eps_sb[:, :], EPS), writes=["eps"])
        sc.step("sync", lambda e: e.dma_start(
            out=bl_sb[:, :], in_=bl_h.rearrange("(c p) -> p c", p=P),
            allow_slow_non_contiguous=True),
            writes=["bl"], sem="qc0", n=16)
        sc.step("sync", lambda e: e.dma_start(
            out=dp1_sb[:, :], in_=dp_h.rearrange("(c p) -> p c", p=P),
            allow_slow_non_contiguous=True),
            writes=["dp1"], sem="qc1", n=16)
        sc.step("scalar", lambda e: e.add(out=dp1_sb[:, :], in_=dp1_sb[:, :], add=1.0),
                reads=["dp1"], writes=["dp1"])

        def ln_group(src_ap_fn, gpar, rd_toks, pair=None):
            sl = gpar % 2
            ptk = pt[sl if pair is None else pair]
            x1tok, stat6, mv = lnb["x1tok"], lnb["stat6"], lnb["mv"]
            sdv, rstd, nmr = lnb["sdv"], lnb["rstd"], lnb["nmr"]
            pr = sl if pair is None else pair
            btoks = [("pb", 4 + 2 * pr), ("pb", 5 + 2 * pr)]
            for dc in range(DC):
                sc.step("tensor", (lambda o, s: (lambda e: e.transpose(o, s, id32[:, :])))(
                    ptk[:, dc * P:(dc + 1) * P], src_ap_fn(dc)),
                    reads=rd_toks(dc) + ["id32"],
                    writes=[btoks[dc // 4]], lazy=(dc < DC - 1))
            for half in range(2):
                sc.step("vector", (lambda o, s: (lambda e: e.bn_stats(out=o, in_=s)))(
                    stat6[sl][:, half * 6:(half + 1) * 6],
                    ptk[:, half * 512:(half + 1) * 512]),
                    reads=[btoks[half]], writes=[("stat", sl)])
            sc.step("vector", (lambda o, s: (lambda e: e.bn_aggr(out=o, in_=s)))(
                mv[sl][:, :], stat6[sl][:, :]),
                reads=[("stat", sl)], writes=[("mv", sl)])
            sc.step("scalar", (lambda o, s: (lambda e: e.activation(
                out=o, in_=s, func=AF.Sqrt, bias=eps_sb[:, 0:1], scale=1.0)))(
                sdv[sl][:, :], mv[sl][:, 1:2]),
                reads=[("mv", sl), "eps"], writes=[("sd", sl)])
            sc.step("vector", (lambda o, s: (lambda e: e.reciprocal(out=o, in_=s)))(
                rstd[sl][:, :], sdv[sl][:, :]),
                reads=[("sd", sl)], writes=[("rstd", sl)])
            sc.step("vector", (lambda o, a, b2: (lambda e: e.tensor_tensor(
                out=o, in0=a, in1=b2, op=OP.mult)))(
                nmr[sl][:, :], mv[sl][:, 0:1], rstd[sl][:, :]),
                reads=[("mv", sl), ("rstd", sl)], writes=[("nmr", sl)])
            sc.step("scalar", (lambda o, s: (lambda e: e.mul(out=o, in_=s, mul=-1.0)))(
                nmr[sl][:, :], nmr[sl][:, :]),
                reads=[("nmr", sl)], writes=[("nmr", sl)])
            sc.step("scalar", (lambda o, s, r, b2: (lambda e: e.activation(
                out=o, in_=s, func=AF.Identity, bias=b2, scale=r)))(
                x1tok[gpar][:, :], ptk[:, :], rstd[sl][:, :], nmr[sl][:, :]),
                reads=btoks + [("rstd", sl), ("nmr", sl)],
                writes=[("x1tokbuf", gpar)])

        gu_items = []
        for (w_src, w_dst) in [(wg_h, wgT_s), (wu_h, wuT_s)]:
            for hc4 in range(HC // 4):
                gu_items.append(("gu4", w_src, w_dst, hc4 * 4))
        wd_items = [("wd", dc) for dc in range(DC)]
        prep_items = []
        for i in range(NST):
            prep_items.append(wd_items[2 * i])
            prep_items.extend(gu_items[3 * i:3 * i + 2])
            prep_items.append(wd_items[2 * i + 1])
            prep_items.append(gu_items[3 * i + 2])

        def emit_prep(items, bufs):
            cv_gu, cv_wd, stg = bufs["cv_gu"], bufs["cv_wd"], bufs["stg"]
            for it in items:
                if it[0] == "gu4":
                    _, w_src, w_dst, hc0 = it
                    sl = (hc0 // 4) % 2
                    sc.step("gpsimd", (lambda d, s: (lambda e: e.dma_start(out=d, in_=s)))(
                        cv_gu[sl].rearrange("p (c j) -> p c j", j=D),
                        w_src[hc0 * P:(hc0 + 4) * P, :].rearrange(
                            "(c p) j -> p c j", p=P)),
                        writes=[("cvgu", sl)], sem=f"qp{sl}", n=16)
                    for q in range(4):
                        hc = hc0 + q
                        for dc in range(DC):
                            sc.step("tensor", (lambda o, s: (lambda e: e.transpose(o, s, idbf[:, :])))(
                                ps[sl][:, dc * 64:(dc + 1) * 64].bitcast(BF16),
                                cv_gu[sl][:, (q * DC + dc) * P:(q * DC + dc) * P + P]),
                                reads=[("cvgu", sl), "idbf"], writes=[("pb", sl)],
                                lazy=(dc < DC - 1))
                        sc.step("scalar", (lambda o, s: (lambda e: e.copy(out=o, in_=s)))(
                            stg[sl][:, q * D:(q + 1) * D],
                            ps[sl][:, 0:512].bitcast(BF16)),
                            reads=[("pb", sl)], writes=[("stg", sl, q)])
                    sc.step("scalar", (lambda d, s: (lambda e: e.dma_start(out=d, in_=s)))(
                        w_dst[hc0:hc0 + 4, :, :].rearrange("h p j -> p h j"),
                        stg[sl].rearrange("p (h j) -> p h j", h=4)),
                        reads=[("stg", sl, q2) for q2 in range(4)],
                        sem=f"qps{sl}", n=16)
                else:
                    _, dc = it
                    sl = dc % 2
                    sc.step("gpsimd", (lambda d, s: (lambda e: e.dma_start(out=d, in_=s)))(
                        cv_wd[sl][:, :], wd_h[dc * P:(dc + 1) * P, :]),
                        writes=[("cvwd", sl)], sem=f"qx{sl}", n=16)
                    stgw = bufs["stgw"]
                    for basehc in range(0, HC, 8):
                        for k in range(8):
                            hc = basehc + k
                            sc.step("tensor", (lambda o, s: (lambda e: e.transpose(o, s, idbf[:, :])))(
                                ps[sl][:, k * 64:(k + 1) * 64].bitcast(BF16),
                                cv_wd[sl][:, hc * P:(hc + 1) * P]),
                                reads=[("cvwd", sl), "idbf"], writes=[("pb", sl)],
                                lazy=(k < 7))
                        ssl = (dc + basehc // 8) % 2
                        sc.step("scalar", (lambda o, s: (lambda e: e.copy(out=o, in_=s)))(
                            stgw[ssl][:, :], ps[sl][:, 0:512].bitcast(BF16)),
                            reads=[("pb", sl)], writes=[("stgw", ssl)])
                        sc.step("scalar", (lambda d, s: (lambda e: e.dma_start(out=d, in_=s)))(
                            wdT_s[basehc:basehc + 8, :, dc * P:(dc + 1) * P].rearrange(
                                "h p j -> p h j"),
                            stgw[ssl][:, :].rearrange("p (h j) -> p h j", h=8)),
                            reads=[("stgw", ssl)], sem=f"qpw{ssl}", n=16)

        with contextlib.ExitStack() as ssm_ctx:
            SE = ssm_ctx.enter_context
            ssm_wlb = SE(nc.sbuf_tensor("ssm_wlb", [P, 4 * D], F32))
            ssm_wc = SE(nc.sbuf_tensor("ssm_wc", [P, NCC * D], F32))
            sc.step("sync", lambda e: e.dma_start(
                out=ssm_wlb[:, 0:2 * D], in_=wl_h.rearrange("(c p) d -> p c d", p=P)),
                writes=["wlbn"], sem="qw1", n=16)
            sc.step("sync", lambda e: e.dma_start(
                out=ssm_wlb[:, 2 * D:4 * D], in_=wb_h.rearrange("(c p) d -> p c d", p=P)),
                writes=["wlbn"], sem="qw2", n=16)
            sc.step("sync", lambda e: e.dma_start(
                out=ssm_wc[:, :], in_=wc_h.rearrange("(c p) n -> p c n", p=P)),
                writes=["wcn"], sem="qw3", n=16)

            tiles = [("wlb", i, dc) for i in range(4) for dc in range(DC)] + \
                    [("wc", ncc, dc) for ncc in range(NCC) for dc in range(DC)]
            for base in range(0, len(tiles), 4):
                grp = tiles[base:base + 4]
                bidx = (base // 4) % 2
                for k, (kind, a, dc) in enumerate(grp):
                    if kind == "wlb":
                        srcap = ssm_wlb[:, (a * DC + dc) * P:(a * DC + dc) * P + P]
                        rtok = "wlbn"
                    else:
                        srcap = ssm_wc[:, dc * N + a * P: dc * N + (a + 1) * P]
                        rtok = "wcn"
                    sc.step("tensor", (lambda o, s: (lambda e: e.transpose(o, s, id32[:, :])))(
                        bank(bidx)[:, k * P:(k + 1) * P], srcap),
                        reads=[rtok, "id32"], writes=[("pb", bidx)],
                        lazy=(k < len(grp) - 1))
                for k, (kind, a, dc) in enumerate(grp):
                    if kind == "wlb":
                        hi = wlbT_hi[:, (a * DC + dc) * P:(a * DC + dc) * P + P]
                        lo = wlbT_lo[:, (a * DC + dc) * P:(a * DC + dc) * P + P]
                        wtok = "wlbT"
                    else:
                        hi = wcT_hi[:, (a * DC + dc) * P:(a * DC + dc) * P + P]
                        lo = wcT_lo[:, (a * DC + dc) * P:(a * DC + dc) * P + P]
                        wtok = "wcT"
                    pbk = bank(bidx)[:, k * P:(k + 1) * P]
                    sc.step("scalar", (lambda o, s: (lambda e: e.copy(out=o, in_=s)))(hi, pbk),
                            reads=[("pb", bidx)], writes=[wtok])
                    sc.step("vector", (lambda o, s, h: (lambda e: e.tensor_tensor(
                        out=o, in0=s, in1=h, op=OP.subtract)))(lo, pbk, hi),
                        reads=[("pb", bidx), wtok], writes=[wtok])
        sc.barrier()

        with contextlib.ExitStack() as a_ctx:
            AE = a_ctx.enter_context
            utok = [AE(nc.sbuf_tensor(f"utok{k}", [P, D], F32)) for k in range(2)]
            uhb = [AE(nc.sbuf_tensor(f"uhb{k}", [P, D], BF16)) for k in range(2)]
            ulb = [AE(nc.sbuf_tensor(f"ulb{k}", [P, D], BF16)) for k in range(2)]
            lam_sb = [AE(nc.sbuf_tensor(f"lam{k}", [P, ST], F32)) for k in range(NCC)]
            Ht = [[AE(nc.sbuf_tensor(f"Ht{pq}_{k}", [P, ST], F32)) for k in range(NCC)]
                  for pq in range(2)]
            u_hiF = AE(nc.sbuf_tensor("u_hiF", [P, DC * TL], BF16))
            u_loF = AE(nc.sbuf_tensor("u_loF", [P, DC * TL], BF16))
            H_hiF = [AE(nc.sbuf_tensor(f"H_hiF{k}", [P, TL], BF16)) for k in range(NCC)]
            H_loF = [AE(nc.sbuf_tensor(f"H_loF{k}", [P, TL], BF16)) for k in range(NCC)]

            prep_ctx = contextlib.ExitStack()
            PE2 = prep_ctx.enter_context
            pbufs = dict(
                cv_gu=[PE2(nc.sbuf_tensor(f"cvgu{k}", [P, 4 * D], BF16)) for k in range(2)],
                cv_wd=[PE2(nc.sbuf_tensor(f"cvwd{k}", [P, HID], BF16)) for k in range(2)],
                stg=[PE2(nc.sbuf_tensor(f"stg{k}", [P, 4 * D], BF16)) for k in range(2)],
                stgw=[PE2(nc.sbuf_tensor(f"stgw{k}", [P, D], BF16)) for k in range(2)],
            )

            nprep = len(prep_items)
            NSL = 2 * NST
            prep_slices = [prep_items[(nprep * i) // NSL:(nprep * (i + 1)) // NSL]
                           for i in range(NSL)]

            def emit_A0(st):
                pre = st < NST
                s4 = st % NST
                src = u_pre if pre else u_main
                for g in range(4):
                    sl = g % 2
                    rows = src[s4 * ST + g * P: s4 * ST + (g + 1) * P, :]
                    sc.step("sync", (lambda d, s: (lambda e: e.dma_start(out=d, in_=s)))(
                        utok[sl][:, :], rows),
                        writes=[("utok", sl)], sem=f"qu{sl}", n=16)
                    sc.step("vector", (lambda o, s: (lambda e: e.tensor_copy(out=o, in_=s)))(
                        uhb[sl][:, :], utok[sl][:, :]),
                        reads=[("utok", sl)], writes=[("uhb", sl)])
                    sc.step("vector", (lambda o, s, h: (lambda e: e.tensor_tensor(
                        out=o, in0=s, in1=h, op=OP.subtract)))(
                        ulb[sl][:, :], utok[sl][:, :], uhb[sl][:, :]),
                        reads=[("utok", sl), ("uhb", sl)], writes=[("ulb", sl)])
                    srow = st * ST + g * P
                    sc.step("scalar", (lambda d, s: (lambda e: e.dma_start(out=d, in_=s)))(
                        u_s[0, srow:srow + P, :], uhb[sl][:, :]),
                        reads=[("uhb", sl)], writes=[("us0", st, g)], sem=f"qus{sl}", n=16)
                    sc.step("scalar", (lambda d, s: (lambda e: e.dma_start(out=d, in_=s)))(
                        u_s[1, srow:srow + P, :], ulb[sl][:, :]),
                        reads=[("ulb", sl)], writes=[("us1", st, g)], sem=f"qul{sl}", n=16)
                lane = f"qt{st}"
                wrdeps = sc.fence(["qus0", "qus1", "qul0", "qul1"])
                for dc in range(DC):
                    for (hl, arena) in ((0, u_hiF), (1, u_loF)):
                        sc.step("sync", (lambda d, s: (lambda e: e.dma_start(
                            out=d, in_=s, transpose=True)))(
                            arena[:, dc * TL + s4 * ST: dc * TL + (s4 + 1) * ST],
                            u_s[hl, st * ST:(st + 1) * ST, dc * P:(dc + 1) * P]),
                            extra_deps=wrdeps, writes=[("uarena", s4)], sem=lane, n=16)

            def emit_mm(st):
                s4 = st % NST
                par = st % 2
                for j in range(4):
                    nmm = DC * 3
                    k = 0
                    for dc in range(DC):
                        rhs_hi = u_hiF[:, dc * TL + s4 * ST: dc * TL + (s4 + 1) * ST]
                        rhs_lo = u_loF[:, dc * TL + s4 * ST: dc * TL + (s4 + 1) * ST]
                        whi = wlbT_hi[:, (j * DC + dc) * P:(j * DC + dc) * P + P]
                        wlo = wlbT_lo[:, (j * DC + dc) * P:(j * DC + dc) * P + P]
                        for (lhsT, rhs, rt) in [(whi, rhs_hi, ("uarena", s4)),
                                                (whi, rhs_lo, ("uarena", s4)),
                                                (wlo, rhs_hi, ("uarena", s4))]:
                            sc.step("tensor", (lambda o, l, r, st_, sp_: (
                                lambda e: e.matmul(o, l, r, start=st_, stop=sp_)))(
                                bank(j), lhsT, rhs, k == 0, k == nmm - 1),
                                reads=[rt, "wlbT"], writes=[("pb", j)],
                                lazy=(k < nmm - 1))
                            k += 1

            def emit_post(st):
                pre = st < NST
                s4 = st % NST
                par = st % 2
                for ncc in range(NCC):
                    sc.step("scalar", (lambda o, s, b2: (lambda e: e.activation(
                        out=o, in_=s, func=AF.Sigmoid, bias=b2, scale=1.0)))(
                        lam_sb[ncc][:, :], bank(ncc), bl_sb[:, ncc:ncc + 1]),
                        reads=[("pb", ncc), "bl"], writes=[("lam", ncc)])
                    init = 0.0 if st == 0 else Ht[(st - 1) % 2][ncc][:, ST - 1:ST]
                    rd = [("lam", ncc), ("pb", 2 + ncc)]
                    if st > 0:
                        rd.append(("Ht", (st - 1) % 2, ncc))
                    sc.step("vector", (lambda o, d0, d1, ini: (lambda e: e.tensor_tensor_scan(
                        out=o, data0=d0, data1=d1, initial=ini, op0=OP.mult, op1=OP.add)))(
                        Ht[par][ncc][:, :], lam_sb[ncc][:, :], bank(2 + ncc), init),
                        reads=rd, writes=[("Ht", par, ncc)])
                    if not pre:
                        hi = H_hiF[ncc][:, s4 * ST:(s4 + 1) * ST]
                        lo = H_loF[ncc][:, s4 * ST:(s4 + 1) * ST]
                        sc.step("scalar", (lambda o, s: (lambda e: e.copy(out=o, in_=s)))(
                            hi, Ht[par][ncc][:, :]),
                            reads=[("Ht", par, ncc)], writes=[("Hshi", ncc, s4)])
                        sc.step("vector", (lambda o, s, h: (lambda e: e.tensor_tensor(
                            out=o, in0=s, in1=h, op=OP.subtract)))(
                            lo, Ht[par][ncc][:, :], hi),
                            reads=[("Ht", par, ncc), ("Hshi", ncc, s4)],
                            writes=[("Hslo", ncc, s4)])



            emit_A0(0)
            for st in range(2 * NST):
                emit_mm(st)
                if st + 1 < 2 * NST:
                    emit_A0(st + 1)
                emit_post(st)
                emit_prep(prep_slices[st], pbufs)

            prep_ctx.close()
            sc.barrier()

            with contextlib.ExitStack() as m_ctx:
                ME = m_ctx.enter_context
                usum = [ME(nc.sbuf_tensor(f"usum{k}", [P, ST], F32)) for k in range(2)]
                t2b = [ME(nc.sbuf_tensor(f"t2b{k}", [P, ST], F32)) for k in range(2)]
                res_stack = contextlib.ExitStack()
                res1 = res_stack.enter_context(
                    nc.sbuf_tensor("res1", [P, DC * TL], F32, side="right"))

                for tt in range(NST):
                    for dc in range(DC):
                        bk = 6 + dc % 2
                        k = 0
                        for ncc in range(NCC):
                            whi = wcT_hi[:, (ncc * DC + dc) * P:(ncc * DC + dc) * P + P]
                            wlo = wcT_lo[:, (ncc * DC + dc) * P:(ncc * DC + dc) * P + P]
                            for (lhsT, rhs, rt) in [
                                (whi, H_hiF[ncc][:, tt * ST:(tt + 1) * ST], ("Hshi", ncc, tt)),
                                (whi, H_loF[ncc][:, tt * ST:(tt + 1) * ST], ("Hslo", ncc, tt)),
                                (wlo, H_hiF[ncc][:, tt * ST:(tt + 1) * ST], ("Hshi", ncc, tt)),
                            ]:
                                sc.step("tensor", (lambda o, l, r, st_, sp_: (
                                    lambda e: e.matmul(o, l, r, start=st_, stop=sp_)))(
                                    bank(bk), lhsT, rhs, k == 0, k == 5),
                                    reads=[rt, "wcT"], writes=[("pb", bk)], lazy=(k < 5))
                                k += 1
                        uh = u_hiF[:, dc * TL + tt * ST: dc * TL + (tt + 1) * ST]
                        ul = u_loF[:, dc * TL + tt * ST: dc * TL + (tt + 1) * ST]
                        sc.step("vector", (lambda o, a, b2: (lambda e: e.tensor_tensor(
                            out=o, in0=a, in1=b2, op=OP.add)))(usum[dc % 2][:, :], uh, ul),
                            reads=[("uarena", tt), ("uarena", tt)], writes=[("usum", dc % 2)])
                        sc.step("scalar", (lambda o, s, scl: (lambda e: e.activation(
                            out=o, in_=s, func=AF.Identity, bias=0.0, scale=scl)))(
                            t2b[dc % 2][:, :], usum[dc % 2][:, :], dp1_sb[:, dc:dc + 1]),
                            reads=[("usum", dc % 2), "dp1"], writes=[("t2b", dc % 2)])
                        sc.step("vector", (lambda o, a, b2: (lambda e: e.tensor_tensor(
                            out=o, in0=a, in1=b2, op=OP.add)))(
                            res1[:, dc * TL + tt * ST: dc * TL + (tt + 1) * ST],
                            t2b[dc % 2][:, :], bank(bk)),
                            reads=[("t2b", dc % 2), ("pb", bk)], writes=[("res1", dc, tt)])
        sc.barrier()

        lnb["x1tok"] = [E(nc.sbuf_tensor(f"x1tok{k}", [P, D], F32)) for k in range(4)]
        lnb["stat6"] = [E(nc.sbuf_tensor(f"stat6_{k}", [P, 12], F32)) for k in range(2)]
        lnb["mv"] = [E(nc.sbuf_tensor(f"mv{k}", [P, 2], F32)) for k in range(2)]
        lnb["sdv"] = [E(nc.sbuf_tensor(f"sdv{k}", [P, 1], F32)) for k in range(2)]
        lnb["rstd"] = [E(nc.sbuf_tensor(f"rstd{k}", [P, 1], F32)) for k in range(2)]
        lnb["nmr"] = [E(nc.sbuf_tensor(f"nmr{k}", [P, 1], F32)) for k in range(2)]
        x1tok = lnb["x1tok"]
        x1hb = [E(nc.sbuf_tensor(f"x1hb{k}", [P, D], BF16)) for k in range(2)]
        x1lb = [E(nc.sbuf_tensor(f"x1lb{k}", [P, D], BF16)) for k in range(2)]
        x1hi = E(nc.sbuf_tensor("x1hi", [P, DC * TL], BF16))
        x1lo = E(nc.sbuf_tensor("x1lo", [P, DC * TL], BF16))

        for s4 in range(NST):
            for g in range(4):
                gg = s4 * 4 + g
                sl = g % 2
                ln_group(lambda dc, gg=gg: res1[:, dc * TL + gg * P: dc * TL + (gg + 1) * P],
                         g, lambda dc, s4=s4: [("res1", dc, s4)])
                sc.step("scalar", (lambda o, sp: (lambda e: e.copy(out=o, in_=sp)))(
                    x1hb[sl][:, :], x1tok[g][:, :]),
                    reads=[("x1tokbuf", g)], writes=[("x1hb", sl)])
                sc.step("vector", (lambda o, sp, h: (lambda e: e.tensor_tensor(
                    out=o, in0=sp, in1=h, op=OP.subtract)))(
                    x1lb[sl][:, :], x1tok[g][:, :], x1hb[sl][:, :]),
                    reads=[("x1tokbuf", g), ("x1hb", sl)], writes=[("x1lb", sl)])
                sc.step("scalar", (lambda d, sp: (lambda e: e.dma_start(out=d, in_=sp)))(
                    x1_s[0, gg * P:(gg + 1) * P, :], x1hb[sl][:, :]),
                    reads=[("x1hb", sl)], sem=f"qxs{sl}", n=16)
                sc.step("scalar", (lambda d, sp: (lambda e: e.dma_start(out=d, in_=sp)))(
                    x1_s[1, gg * P:(gg + 1) * P, :], x1lb[sl][:, :]),
                    reads=[("x1lb", sl)], sem=f"qxl{sl}", n=16)
            wrdeps = sc.fence(["qxs0", "qxs1", "qxl0", "qxl1"])
            lane = f"qxt{s4}"
            for dc in range(DC):
                for (hl, arena) in ((0, x1hi), (1, x1lo)):
                    sc.step("sync", (lambda d, sp: (lambda e: e.dma_start(
                        out=d, in_=sp, transpose=True)))(
                        arena[:, dc * TL + s4 * ST: dc * TL + (s4 + 1) * ST],
                        x1_s[hl, s4 * ST:(s4 + 1) * ST, dc * P:(dc + 1) * P]),
                        extra_deps=wrdeps, writes=[("x1arena", s4)], sem=lane, n=16)

        res_stack.close()
        sc.barrier()
        prep_fence = sc.fence(["qps0", "qps1", "qpw0", "qpw1"])

        wbufg = [E(nc.sbuf_tensor(f"wbufg{k}", [P, D], BF16)) for k in range(2)]
        wbufu = [E(nc.sbuf_tensor(f"wbufu{k}", [P, D], BF16)) for k in range(2)]
        wbufd = [E(nc.sbuf_tensor(f"wbufd{k}", [P, D], BF16)) for k in range(2)]
        g_sb = [E(nc.sbuf_tensor(f"g_sb{k}", [P, ST], F32)) for k in range(2)]
        hhf = [E(nc.sbuf_tensor(f"hhf{k}", [P, ST], F32)) for k in range(2)]
        x1s = [E(nc.sbuf_tensor(f"x1s{k}", [P, ST], F32)) for k in range(2)]
        res2 = E(nc.sbuf_tensor("res2", [P, DC * ST], F32))
        hh_hi = E(nc.sbuf_tensor("hh_hi", [P, HC * ST], BF16, side="right"))
        hh_lo = E(nc.sbuf_tensor("hh_lo", [P, HC * ST], BF16, side="right"))

        GB = [0, 1, 2]
        UB = [3, 4, 5]

        for m in range(NST):
            for hc in range(HC):
                sl = hc % 2
                ed = prep_fence if (m == 0 and hc < 2) else ()
                sc.step("sync", (lambda d, s: (lambda e: e.dma_start(out=d, in_=s)))(
                    wbufg[sl][:, :], wgT_s[hc, :, :]),
                    writes=[("wbufg", sl)], extra_deps=ed, sem=f"qg{sl}", n=16)
                sc.step("sync", (lambda d, s: (lambda e: e.dma_start(out=d, in_=s)))(
                    wbufu[sl][:, :], wuT_s[hc, :, :]),
                    writes=[("wbufu", sl)], extra_deps=ed, sem=f"qq{sl}", n=16)
                bg, bu_ = GB[hc % 3], UB[hc % 3]
                for (wb2, bb, tok) in [(wbufg[sl], bg, ("wbufg", sl)),
                                       (wbufu[sl], bu_, ("wbufu", sl))]:
                    k = 0
                    for dc in range(DC):
                        for (rhs, rt) in [
                            (x1hi[:, dc * TL + m * ST: dc * TL + (m + 1) * ST],
                             ("x1arena", m)),
                            (x1lo[:, dc * TL + m * ST: dc * TL + (m + 1) * ST],
                             ("x1arena", m)),
                        ]:
                            sc.step("tensor", (lambda o, l, r, st_, sp_: (
                                lambda e: e.matmul(o, l, r, start=st_, stop=sp_)))(
                                bank(bb), wb2[:, dc * P:(dc + 1) * P], rhs,
                                k == 0, k == 2 * DC - 1),
                                reads=[rt, tok], writes=[("pb", bb)],
                                lazy=(k < 2 * DC - 1))
                            k += 1
                sc.step("scalar", (lambda o, s: (lambda e: e.activation(
                    out=o, in_=s, func=AF.Silu, bias=0.0, scale=SCALE)))(
                    g_sb[sl][:, :], bank(bg)),
                    reads=[("pb", bg)], writes=[("g_sb", sl)])
                sc.step("vector", (lambda o, a, b2: (lambda e: e.tensor_tensor(
                    out=o, in0=a, in1=b2, op=OP.mult)))(
                    hhf[sl][:, :], g_sb[sl][:, :], bank(bu_)),
                    reads=[("g_sb", sl), ("pb", bu_)], writes=[("hhf", sl)])
                hi = hh_hi[:, hc * ST:(hc + 1) * ST]
                lo = hh_lo[:, hc * ST:(hc + 1) * ST]
                sc.step("scalar", (lambda o, s: (lambda e: e.mul(out=o, in_=s, mul=SCALE)))(
                    hi, hhf[sl][:, :]),
                    reads=[("hhf", sl)], writes=[("hhhi", hc)])
                sc.step("vector", (lambda o, s, h: (lambda e: e.scalar_tensor_tensor(
                    out=o, in0=s, scalar=SCALE, in1=h, op0=OP.mult, op1=OP.subtract)))(
                    lo, hhf[sl][:, :], hi),
                    reads=[("hhf", sl), ("hhhi", hc)], writes=[("hhlo", hc)])

            for hc in range(HC):
                sl = hc % 2
                sc.step("gpsimd", (lambda d, s: (lambda e: e.dma_start(out=d, in_=s)))(
                    wbufd[sl][:, :], wdT_s[hc, :, :]),
                    writes=[("wbufd", sl)],
                    extra_deps=prep_fence if (m == 0 and hc == 0) else (),
                    sem=f"qd{sl}", n=16)
                for dc in range(DC):
                    for pi, (rhs, rt) in enumerate([
                        (hh_hi[:, hc * ST:(hc + 1) * ST], ("hhhi", hc)),
                        (hh_lo[:, hc * ST:(hc + 1) * ST], ("hhlo", hc)),
                    ]):
                        k = hc * 2 + pi
                        sc.step("tensor", (lambda o, l, r, st_, sp_: (
                            lambda e: e.matmul(o, l, r, start=st_, stop=sp_)))(
                            bank(dc), wbufd[sl][:, dc * P:(dc + 1) * P], rhs,
                            k == 0, k == 2 * HC - 1),
                            reads=[rt, ("wbufd", sl)], writes=[("pb", dc)],
                            lazy=not (dc == DC - 1 and pi == 1))

            for dc in range(DC):
                sl = dc % 2
                uh = x1hi[:, dc * TL + m * ST: dc * TL + (m + 1) * ST]
                ul = x1lo[:, dc * TL + m * ST: dc * TL + (m + 1) * ST]
                sc.step("vector", (lambda o, a, b2: (lambda e: e.tensor_tensor(
                    out=o, in0=a, in1=b2, op=OP.add)))(x1s[sl][:, :], uh, ul),
                    reads=[("x1arena", m), ("x1arena", m)], writes=[("x1s", sl)])
                sc.step("vector", (lambda o, s, a: (lambda e: e.scalar_tensor_tensor(
                    out=o, in0=s, scalar=SCALE, in1=a, op0=OP.mult, op1=OP.add)))(
                    res2[:, dc * ST:(dc + 1) * ST], bank(dc), x1s[sl][:, :]),
                    reads=[("pb", dc), ("x1s", sl)], writes=[("res2", dc)])

            for g in range(4):
                gg = m * 4 + g
                sl = g % 2
                ln_group(lambda dc, g=g: res2[:, dc * ST + g * P: dc * ST + (g + 1) * P],
                         sl, lambda dc: [("res2", dc)])
                sc.step("gpsimd", (lambda d, s: (lambda e: e.dma_start(out=d, in_=s)))(
                    out_h[gg * P:(gg + 1) * P, :], x1tok[sl][:, :]),
                    reads=[("x1tokbuf", sl)], sem=f"qo{sl}", n=16)

        qo_tot = (sc.seq.get("qo0", 0), sc.seq.get("qo1", 0))

        @block.tensor
        def _(eng):
            sc.emit("tensor", eng, sems)

        @block.scalar
        def _(eng):
            sc.emit("scalar", eng, sems)

        @block.vector
        def _(eng):
            sc.emit("vector", eng, sems)

        @block.gpsimd
        def _(eng):
            sc.emit("gpsimd", eng, sems)
            eng.wait_ge(sems["qo0"], qo_tot[0])
            eng.wait_ge(sems["qo1"], qo_tot[1])

        @block.sync
        def _(eng):
            sc.emit("sync", eng, sems)
            eng.wait_ge(sems["qo0"], qo_tot[0])
            eng.wait_ge(sems["qo1"], qo_tot[1])

    return nc


_NC_CACHE = [None]


def kernel(L0_attn_u, L0_attn_Wl, L0_attn_bl, L0_attn_Wb, L0_attn_Wc, L0_attn_Dp,
           L0_ffn_w_gate, L0_ffn_w_up, L0_ffn_w_down):
    u = np.ascontiguousarray(np.asarray(L0_attn_u, dtype=np.float32))
    shared = dict(
        wl=np.ascontiguousarray(np.asarray(L0_attn_Wl, np.float32)),
        bl=np.ascontiguousarray(np.asarray(L0_attn_bl, np.float32)),
        wb=np.ascontiguousarray(np.asarray(L0_attn_Wb, np.float32)),
        wc=np.ascontiguousarray(np.asarray(L0_attn_Wc, np.float32)),
        dp=np.ascontiguousarray(np.asarray(L0_attn_Dp, np.float32)),
        wg=np.ascontiguousarray(np.asarray(L0_ffn_w_gate, np.int32)),
        wu=np.ascontiguousarray(np.asarray(L0_ffn_w_up, np.int32)),
        wd=np.ascontiguousarray(np.asarray(L0_ffn_w_down, np.int32)),
    )
    zeros = np.zeros((TL, D), np.float32)
    in_maps = []
    for c in range(8):
        b, h = divmod(c, 2)
        mmap = dict(shared)
        mmap["u_main"] = np.ascontiguousarray(u[b, h * TL:(h + 1) * TL])
        mmap["u_pre"] = np.ascontiguousarray(u[b, 0:TL]) if h == 1 else zeros
        in_maps.append(mmap)

    if _NC_CACHE[0] is None:
        _NC_CACHE[0] = build_nc()
    nc = _NC_CACHE[0]
    res = run_bass_kernel_spmd(nc, in_maps, core_ids=list(range(8)))

    out = np.empty((B, T, D), np.float32)
    for c in range(8):
        b, h = divmod(c, 2)
        out[b, h * TL:(h + 1) * TL] = res.results[c]["out"]
    return out


# version 13
# speedup vs baseline: 1.1817x; 1.1817x over previous
import contextlib
import numpy as np

import concourse.bass as bass
import concourse.mybir as mybir
from concourse.bass_utils import run_bass_kernel_spmd

F32 = mybir.dt.float32
BF16 = mybir.dt.bfloat16
I32 = mybir.dt.int32
AF = mybir.ActivationFunctionType
OP = mybir.AluOpType

P = 128
B, T, D, N, HID = 4, 4096, 1024, 256, 3072
TL = T // 2
ST = 512
NST = TL // ST
DC = D // P
NCC = N // P
HC = HID // P
SCALE = float(np.float32(1.0 / 127.0))
EPS = 1e-5


class Sched:

    def __init__(self):
        self.steps = []
        self.seq = {}
        self.last_w = {}
        self.readers = {}
        self.emitted_wait = {}
        self.dma_sems = set()
        self.pending = {}

    def step(self, sec, fn, reads=(), writes=(), extra_deps=(), sem=None,
             n=1, lazy=False):
        if sem is None:
            sem = sec
        pf = getattr(self, "pending_fence", None)
        if pf and sec in pf:
            extra_deps = list(extra_deps) + pf.pop(sec)
        if n == 16:
            self.dma_sems.add(sem)
        deps = {}

        def add(dep):
            if dep is None:
                return
            e, v = dep
            if v == "PENDING":
                assert e == sem, (
                    f"dependency on a lazy step's output via sem {e} from a "
                    f"step on sem {sem}; make that producer non-lazy")
                return
            if deps.get(e, 0) < v:
                deps[e] = v

        for t in reads:
            add(self.last_w.get(t))
        for t in writes:
            add(self.last_w.get(t))
            for r in self.readers.get(t, ()):
                add(r)
        for d in extra_deps:
            add(d)

        if sec == "tensor":
            deps.pop("tensor", None)

        for (e, v) in deps.items():
            if e in self.dma_sems:
                assert v == self.seq.get(e, 0), (
                    f"stale DMA lane wait: need {e}>={v}, lane at "
                    f"{self.seq.get(e, 0)}")

        waits = []
        for (e, v) in sorted(deps.items(), key=str):
            key = (sec, e)
            if self.emitted_wait.get(key, 0) >= v:
                continue
            self.emitted_wait[key] = v
            waits.append((e, v))

        if lazy:
            assert sec == "tensor", "lazy steps are tensor-only"
            tok = (sem, "PENDING")
            self.steps.append((sec, waits, fn, None, 0))
            pend = self.pending.setdefault(sem, [])
            for t in reads:
                pend.append(("r", t))
            for t in writes:
                self.last_w[t] = tok
                self.readers[t] = []
                pend.append(("w", t))
            return tok

        val_after = self.seq.get(sem, 0) + n
        self.seq[sem] = val_after
        self.steps.append((sec, waits, fn, sem, n))
        tok = (sem, val_after)
        for kind, t in self.pending.pop(sem, ()):
            if kind == "w":
                if self.last_w.get(t, (None, None))[1] == "PENDING":
                    self.last_w[t] = tok
            else:
                self.readers.setdefault(t, []).append(tok)
        for t in reads:
            self.readers.setdefault(t, []).append(tok)
        for t in writes:
            self.last_w[t] = tok
            self.readers[t] = []
        return tok

    def fence(self, sems_):
        return [(s, self.seq.get(s, 0)) for s in sems_]

    def barrier(self, exclude=()):
        f = self.fence([k for k in self.seq.keys() if k not in exclude])
        self.pending_fence = {s: list(f) for s in
                              ("tensor", "scalar", "vector", "gpsimd", "sync")}

    def emit(self, name, eng_obj, semmap):
        for (sec, waits, fn, sem, n) in self.steps:
            if sec != name:
                continue
            for (s, v) in waits:
                eng_obj.wait_ge(semmap[s], v)
            inst = fn(eng_obj)
            if sem is not None:
                inst.then_inc(semmap[sem], n)


def build_nc():
    nc = bass.Bass("TRN2", target_bir_lowering=False)

    u_main = nc.dram_tensor("u_main", [TL, D], F32, kind="ExternalInput").ap()
    u_pre = nc.dram_tensor("u_pre", [TL, D], F32, kind="ExternalInput").ap()
    wl_h = nc.dram_tensor("wl", [N, D], F32, kind="ExternalInput").ap()
    bl_h = nc.dram_tensor("bl", [N], F32, kind="ExternalInput").ap()
    wb_h = nc.dram_tensor("wb", [N, D], F32, kind="ExternalInput").ap()
    wc_h = nc.dram_tensor("wc", [D, N], F32, kind="ExternalInput").ap()
    dp_h = nc.dram_tensor("dp", [D], F32, kind="ExternalInput").ap()
    wg_h = nc.dram_tensor("wg", [HID, D], I32, kind="ExternalInput").ap()
    wu_h = nc.dram_tensor("wu", [HID, D], I32, kind="ExternalInput").ap()
    wd_h = nc.dram_tensor("wd", [D, HID], I32, kind="ExternalInput").ap()
    out_h = nc.dram_tensor("out", [TL, D], F32, kind="ExternalOutput").ap()

    wgT_s = nc.dram_tensor("wgT_s", [HC, P, D], BF16).ap()
    wuT_s = nc.dram_tensor("wuT_s", [HC, P, D], BF16).ap()
    wdT_s = nc.dram_tensor("wdT_s", [HC, P, D], BF16).ap()

    sc = Sched()
    sem_names = ["tensor", "scalar", "vector", "gpsimd",
                 "qc0", "qc1", "qu0", "qu1", "qp0", "qp1", "qps0", "qps1",
                 "qw0", "qw1", "qw2", "qw3", "qx0", "qx1", "qpw0", "qpw1",
                 "qg0", "qg1", "qq0", "qq1", "qd0", "qd1", "qo0", "qo1"]

    with contextlib.ExitStack() as ctx:
        E = ctx.enter_context

        id32 = E(nc.sbuf_tensor("id32", [P, P], F32))
        idbf = E(nc.sbuf_tensor("idbf", [P, P], BF16))
        eps_sb = E(nc.sbuf_tensor("eps_sb", [P, 1], F32))
        bl_sb = E(nc.sbuf_tensor("bl_sb", [P, NCC], F32))
        dp1_sb = E(nc.sbuf_tensor("dp1_sb", [P, DC], F32))
        wlbT_hi = E(nc.sbuf_tensor("wlbT_hi", [P, 4 * D], BF16))
        wlbT_lo = E(nc.sbuf_tensor("wlbT_lo", [P, 4 * D], BF16))
        wcT_hi = E(nc.sbuf_tensor("wcT_hi", [P, NCC * D], BF16))
        wcT_lo = E(nc.sbuf_tensor("wcT_lo", [P, NCC * D], BF16))

        ps = [E(nc.psum_tensor(f"ps{i}", [P, 512], F32)) for i in range(4)]
        pt = [E(nc.psum_tensor(f"pt{i}", [P, 1024], F32)) for i in range(2)]

        def bank(i):
            if i < 4:
                return ps[i][:, 0:512]
            j, half = divmod(i - 4, 2)
            return pt[j][:, half * 512:(half + 1) * 512]

        sems = {s: E(nc.semaphore(s)) for s in sem_names}
        block = E(nc.Block())

        lnb = {}

        sc.step("gpsimd", lambda e: e.memset(id32[:, :], 0.0), writes=["id32"])
        sc.step("gpsimd", lambda e: e.affine_select(
            out=id32[:, :], in_=id32[:, :], compare_op=OP.not_equal, fill=1.0,
            base=0, pattern=[[-1, P]], channel_multiplier=1),
            reads=["id32"], writes=["id32"])
        sc.step("gpsimd", lambda e: e.memset(idbf[:, :], 0.0), writes=["idbf"])
        sc.step("gpsimd", lambda e: e.affine_select(
            out=idbf[:, :], in_=idbf[:, :], compare_op=OP.not_equal, fill=1.0,
            base=0, pattern=[[-1, P]], channel_multiplier=1),
            reads=["idbf"], writes=["idbf"])
        sc.step("gpsimd", lambda e: e.memset(eps_sb[:, :], EPS), writes=["eps"])
        sc.step("sync", lambda e: e.dma_start(
            out=bl_sb[:, :], in_=bl_h.rearrange("(c p) -> p c", p=P),
            allow_slow_non_contiguous=True),
            writes=["bl"], sem="qc0", n=16)
        sc.step("sync", lambda e: e.dma_start(
            out=dp1_sb[:, :], in_=dp_h.rearrange("(c p) -> p c", p=P),
            allow_slow_non_contiguous=True),
            writes=["dp1"], sem="qc1", n=16)
        sc.step("scalar", lambda e: e.add(out=dp1_sb[:, :], in_=dp1_sb[:, :], add=1.0),
                reads=["dp1"], writes=["dp1"])

        def ln_group(src_ap_fn, gpar, rd_toks, pair=None):
            sl = gpar % 2
            ptk = pt[sl if pair is None else pair]
            x1tok, stat6, mv = lnb["x1tok"], lnb["stat6"], lnb["mv"]
            sdv, rstd, nmr = lnb["sdv"], lnb["rstd"], lnb["nmr"]
            pr = sl if pair is None else pair
            btoks = [("pb", 4 + 2 * pr), ("pb", 5 + 2 * pr)]
            for dc in range(DC):
                sc.step("tensor", (lambda o, s: (lambda e: e.transpose(o, s, id32[:, :])))(
                    ptk[:, dc * P:(dc + 1) * P], src_ap_fn(dc)),
                    reads=rd_toks(dc) + ["id32"],
                    writes=[btoks[dc // 4]], lazy=(dc < DC - 1))
            for half in range(2):
                sc.step("vector", (lambda o, s: (lambda e: e.bn_stats(out=o, in_=s)))(
                    stat6[sl][:, half * 6:(half + 1) * 6],
                    ptk[:, half * 512:(half + 1) * 512]),
                    reads=[btoks[half]], writes=[("stat", sl)])
            sc.step("vector", (lambda o, s: (lambda e: e.bn_aggr(out=o, in_=s)))(
                mv[sl][:, :], stat6[sl][:, :]),
                reads=[("stat", sl)], writes=[("mv", sl)])
            sc.step("scalar", (lambda o, s: (lambda e: e.activation(
                out=o, in_=s, func=AF.Sqrt, bias=eps_sb[:, 0:1], scale=1.0)))(
                sdv[sl][:, :], mv[sl][:, 1:2]),
                reads=[("mv", sl), "eps"], writes=[("sd", sl)])
            sc.step("vector", (lambda o, s: (lambda e: e.reciprocal(out=o, in_=s)))(
                rstd[sl][:, :], sdv[sl][:, :]),
                reads=[("sd", sl)], writes=[("rstd", sl)])
            sc.step("vector", (lambda o, a, b2: (lambda e: e.tensor_tensor(
                out=o, in0=a, in1=b2, op=OP.mult)))(
                nmr[sl][:, :], mv[sl][:, 0:1], rstd[sl][:, :]),
                reads=[("mv", sl), ("rstd", sl)], writes=[("nmr", sl)])
            sc.step("scalar", (lambda o, s: (lambda e: e.mul(out=o, in_=s, mul=-1.0)))(
                nmr[sl][:, :], nmr[sl][:, :]),
                reads=[("nmr", sl)], writes=[("nmr", sl)])
            sc.step("scalar", (lambda o, s, r, b2: (lambda e: e.activation(
                out=o, in_=s, func=AF.Identity, bias=b2, scale=r)))(
                x1tok[gpar][:, :], ptk[:, :], rstd[sl][:, :], nmr[sl][:, :]),
                reads=btoks + [("rstd", sl), ("nmr", sl)],
                writes=[("x1tokbuf", gpar)])

        gu_items = []
        for (w_src, w_dst) in [(wg_h, wgT_s), (wu_h, wuT_s)]:
            for hc4 in range(HC // 4):
                gu_items.append(("gu4", w_src, w_dst, hc4 * 4))
        wd_items = [("wd", dc) for dc in range(DC)]
        prep_items = []
        for i in range(NST):
            prep_items.append(wd_items[2 * i])
            prep_items.extend(gu_items[3 * i:3 * i + 2])
            prep_items.append(wd_items[2 * i + 1])
            prep_items.append(gu_items[3 * i + 2])

        def emit_prep(items, bufs):
            cv_gu, cv_wd, stg = bufs["cv_gu"], bufs["cv_wd"], bufs["stg"]
            for it in items:
                if it[0] == "gu4":
                    _, w_src, w_dst, hc0 = it
                    sl = (hc0 // 4) % 2
                    sc.step("gpsimd", (lambda d, s: (lambda e: e.dma_start(out=d, in_=s)))(
                        cv_gu[sl].rearrange("p (c j) -> p c j", j=D),
                        w_src[hc0 * P:(hc0 + 4) * P, :].rearrange(
                            "(c p) j -> p c j", p=P)),
                        writes=[("cvgu", sl)], sem=f"qp{sl}", n=16)
                    for q in range(4):
                        hc = hc0 + q
                        for dc in range(DC):
                            sc.step("tensor", (lambda o, s: (lambda e: e.transpose(o, s, idbf[:, :])))(
                                ps[sl][:, dc * 64:(dc + 1) * 64].bitcast(BF16),
                                cv_gu[sl][:, (q * DC + dc) * P:(q * DC + dc) * P + P]),
                                reads=[("cvgu", sl), "idbf"], writes=[("pb", sl)],
                                lazy=(dc < DC - 1))
                        sc.step("scalar", (lambda o, s: (lambda e: e.copy(out=o, in_=s)))(
                            stg[sl][:, q * D:(q + 1) * D],
                            ps[sl][:, 0:512].bitcast(BF16)),
                            reads=[("pb", sl)], writes=[("stg", sl, q)])
                    sc.step("scalar", (lambda d, s: (lambda e: e.dma_start(out=d, in_=s)))(
                        w_dst[hc0:hc0 + 4, :, :].rearrange("h p j -> p h j"),
                        stg[sl].rearrange("p (h j) -> p h j", h=4)),
                        reads=[("stg", sl, q2) for q2 in range(4)],
                        sem=f"qps{sl}", n=16)
                else:
                    _, dc = it
                    sl = dc % 2
                    sc.step("gpsimd", (lambda d, s: (lambda e: e.dma_start(out=d, in_=s)))(
                        cv_wd[sl][:, :], wd_h[dc * P:(dc + 1) * P, :]),
                        writes=[("cvwd", sl)], sem=f"qx{sl}", n=16)
                    stgw = bufs["stgw"]
                    for basehc in range(0, HC, 8):
                        for k in range(8):
                            hc = basehc + k
                            sc.step("tensor", (lambda o, s: (lambda e: e.transpose(o, s, idbf[:, :])))(
                                ps[2 + sl][:, k * 64:(k + 1) * 64].bitcast(BF16),
                                cv_wd[sl][:, hc * P:(hc + 1) * P]),
                                reads=[("cvwd", sl), "idbf"], writes=[("pb", 2 + sl)],
                                lazy=(k < 7))
                        ssl = (dc + basehc // 8) % 2
                        sc.step("scalar", (lambda o, s: (lambda e: e.copy(out=o, in_=s)))(
                            stgw[ssl][:, :], ps[2 + sl][:, 0:512].bitcast(BF16)),
                            reads=[("pb", 2 + sl)], writes=[("stgw", ssl)])
                        sc.step("scalar", (lambda d, s: (lambda e: e.dma_start(out=d, in_=s)))(
                            wdT_s[basehc:basehc + 8, :, dc * P:(dc + 1) * P].rearrange(
                                "h p j -> p h j"),
                            stgw[ssl][:, :].rearrange("p (h j) -> p h j", h=8)),
                            reads=[("stgw", ssl)], sem=f"qpw{ssl}", n=16)

        with contextlib.ExitStack() as ssm_ctx:
            SE = ssm_ctx.enter_context
            ssm_wlb = SE(nc.sbuf_tensor("ssm_wlb", [P, 4 * D], F32))
            ssm_wc = SE(nc.sbuf_tensor("ssm_wc", [P, NCC * D], F32))
            sc.step("sync", lambda e: e.dma_start(
                out=ssm_wlb[:, 0:2 * D], in_=wl_h.rearrange("(c p) d -> p c d", p=P)),
                writes=["wlbn"], sem="qw1", n=16)
            sc.step("sync", lambda e: e.dma_start(
                out=ssm_wlb[:, 2 * D:4 * D], in_=wb_h.rearrange("(c p) d -> p c d", p=P)),
                writes=["wlbn"], sem="qw2", n=16)
            sc.step("sync", lambda e: e.dma_start(
                out=ssm_wc[:, :], in_=wc_h.rearrange("(c p) n -> p c n", p=P)),
                writes=["wcn"], sem="qw3", n=16)

            tiles = [("wlb", i, dc) for i in range(4) for dc in range(DC)] + \
                    [("wc", ncc, dc) for ncc in range(NCC) for dc in range(DC)]
            for base in range(0, len(tiles), 4):
                grp = tiles[base:base + 4]
                bidx = (base // 4) % 2
                for k, (kind, a, dc) in enumerate(grp):
                    if kind == "wlb":
                        srcap = ssm_wlb[:, (a * DC + dc) * P:(a * DC + dc) * P + P]
                        rtok = "wlbn"
                    else:
                        srcap = ssm_wc[:, dc * N + a * P: dc * N + (a + 1) * P]
                        rtok = "wcn"
                    sc.step("tensor", (lambda o, s: (lambda e: e.transpose(o, s, id32[:, :])))(
                        bank(bidx)[:, k * P:(k + 1) * P], srcap),
                        reads=[rtok, "id32"], writes=[("pb", bidx)],
                        lazy=(k < len(grp) - 1))
                for k, (kind, a, dc) in enumerate(grp):
                    if kind == "wlb":
                        hi = wlbT_hi[:, (a * DC + dc) * P:(a * DC + dc) * P + P]
                        lo = wlbT_lo[:, (a * DC + dc) * P:(a * DC + dc) * P + P]
                        wtok = "wlbT"
                    else:
                        hi = wcT_hi[:, (a * DC + dc) * P:(a * DC + dc) * P + P]
                        lo = wcT_lo[:, (a * DC + dc) * P:(a * DC + dc) * P + P]
                        wtok = "wcT"
                    pbk = bank(bidx)[:, k * P:(k + 1) * P]
                    sc.step("scalar", (lambda o, s: (lambda e: e.copy(out=o, in_=s)))(hi, pbk),
                            reads=[("pb", bidx)], writes=[wtok])
                    sc.step("vector", (lambda o, s, h: (lambda e: e.tensor_tensor(
                        out=o, in0=s, in1=h, op=OP.subtract)))(lo, pbk, hi),
                        reads=[("pb", bidx), wtok], writes=[wtok])
        sc.barrier()

        with contextlib.ExitStack() as a_ctx:
            AE = a_ctx.enter_context
            utok = [AE(nc.sbuf_tensor(f"utok{k}", [P, D], F32)) for k in range(2)]
            lam_sb = [AE(nc.sbuf_tensor(f"lam{k}", [P, ST], F32)) for k in range(NCC)]
            Ht = [[AE(nc.sbuf_tensor(f"Ht{pq}_{k}", [P, ST], F32)) for k in range(NCC)]
                  for pq in range(2)]
            u_hiF = AE(nc.sbuf_tensor("u_hiF", [P, DC * TL], BF16))
            u_loF = AE(nc.sbuf_tensor("u_loF", [P, DC * TL], BF16))
            H_hiF = [AE(nc.sbuf_tensor(f"H_hiF{k}", [P, TL], BF16)) for k in range(NCC)]
            H_loF = [AE(nc.sbuf_tensor(f"H_loF{k}", [P, TL], BF16)) for k in range(NCC)]

            prep_ctx = contextlib.ExitStack()
            PE2 = prep_ctx.enter_context
            pbufs = dict(
                cv_gu=[PE2(nc.sbuf_tensor(f"cvgu{k}", [P, 4 * D], BF16)) for k in range(2)],
                cv_wd=[PE2(nc.sbuf_tensor(f"cvwd{k}", [P, HID], BF16)) for k in range(2)],
                stg=[PE2(nc.sbuf_tensor(f"stg{k}", [P, 4 * D], BF16)) for k in range(2)],
                stgw=[PE2(nc.sbuf_tensor(f"stgw{k}", [P, D], BF16)) for k in range(2)],
            )

            nprep = len(prep_items)
            NSL = 2 * NST
            prep_slices = [prep_items[(nprep * i) // NSL:(nprep * (i + 1)) // NSL]
                           for i in range(NSL)]

            def emit_A0(st):
                pre = st < NST
                s4 = st % NST
                par = st % 2
                src = u_pre if pre else u_main
                for g in range(4):
                    sl = g % 2
                    rows = src[s4 * ST + g * P: s4 * ST + (g + 1) * P, :]
                    sc.step("sync", (lambda d, s: (lambda e: e.dma_start(out=d, in_=s)))(
                        utok[sl][:, :], rows),
                        writes=[("utok", sl)], sem=f"qu{sl}", n=16)
                    for dc in range(DC):
                        sc.step("tensor", (lambda o, s: (lambda e: e.transpose(o, s, id32[:, :])))(
                            pt[g % 2][:, dc * P:(dc + 1) * P],
                            utok[sl][:, dc * P:(dc + 1) * P]),
                            reads=[("utok", sl), "id32"],
                            writes=[("pb", 4 + 2 * (g % 2) + dc // 4)],
                            lazy=(dc < DC - 1))
                    for half in range(2):
                        bsel = 4 + 2 * (g % 2) + half
                        coff = s4 * ST + g * P
                        hidst = u_hiF.rearrange("p (c j) -> p c j", j=TL)[
                            :, half * 4:(half + 1) * 4, coff:coff + P]
                        lodst = u_loF.rearrange("p (c j) -> p c j", j=TL)[
                            :, half * 4:(half + 1) * 4, coff:coff + P]
                        srcap = pt[g % 2][:, half * 512:(half + 1) * 512
                                          ].rearrange("p (c j) -> p c j", j=P)
                        sc.step("scalar", (lambda o, s: (lambda e: e.copy(out=o, in_=s)))(
                            hidst, srcap),
                            reads=[("pb", bsel)], writes=[("uhit", s4)])
                        sc.step("vector", (lambda o, s, h: (lambda e: e.tensor_tensor(
                            out=o, in0=s, in1=h, op=OP.subtract)))(
                            lodst, srcap, hidst),
                            reads=[("pb", bsel), ("uhit", s4)], writes=[("ulot", s4)])


            def emit_mm(st):
                s4 = st % NST
                par = st % 2
                for j in range(4):
                    nmm = DC * 3
                    k = 0
                    for dc in range(DC):
                        rhs_hi = u_hiF[:, dc * TL + s4 * ST: dc * TL + (s4 + 1) * ST]
                        rhs_lo = u_loF[:, dc * TL + s4 * ST: dc * TL + (s4 + 1) * ST]
                        whi = wlbT_hi[:, (j * DC + dc) * P:(j * DC + dc) * P + P]
                        wlo = wlbT_lo[:, (j * DC + dc) * P:(j * DC + dc) * P + P]
                        for (lhsT, rhs, rt) in [(whi, rhs_hi, ("uhit", s4)),
                                                (whi, rhs_lo, ("ulot", s4)),
                                                (wlo, rhs_hi, ("uhit", s4))]:
                            sc.step("tensor", (lambda o, l, r, st_, sp_: (
                                lambda e: e.matmul(o, l, r, start=st_, stop=sp_)))(
                                bank(j), lhsT, rhs, k == 0, k == nmm - 1),
                                reads=[rt, "wlbT"], writes=[("pb", j)],
                                lazy=(k < nmm - 1))
                            k += 1

            def emit_post(st):
                pre = st < NST
                s4 = st % NST
                par = st % 2
                for ncc in range(NCC):
                    sc.step("scalar", (lambda o, s, b2: (lambda e: e.activation(
                        out=o, in_=s, func=AF.Sigmoid, bias=b2, scale=1.0)))(
                        lam_sb[ncc][:, :], bank(ncc), bl_sb[:, ncc:ncc + 1]),
                        reads=[("pb", ncc), "bl"], writes=[("lam", ncc)])
                    init = 0.0 if st == 0 else Ht[(st - 1) % 2][ncc][:, ST - 1:ST]
                    rd = [("lam", ncc), ("pb", 2 + ncc)]
                    if st > 0:
                        rd.append(("Ht", (st - 1) % 2, ncc))
                    sc.step("vector", (lambda o, d0, d1, ini: (lambda e: e.tensor_tensor_scan(
                        out=o, data0=d0, data1=d1, initial=ini, op0=OP.mult, op1=OP.add)))(
                        Ht[par][ncc][:, :], lam_sb[ncc][:, :], bank(2 + ncc), init),
                        reads=rd, writes=[("Ht", par, ncc)])
                    if not pre:
                        hi = H_hiF[ncc][:, s4 * ST:(s4 + 1) * ST]
                        lo = H_loF[ncc][:, s4 * ST:(s4 + 1) * ST]
                        sc.step("scalar", (lambda o, s: (lambda e: e.copy(out=o, in_=s)))(
                            hi, Ht[par][ncc][:, :]),
                            reads=[("Ht", par, ncc)], writes=[("Hshi", ncc, s4)])
                        sc.step("vector", (lambda o, s, h: (lambda e: e.tensor_tensor(
                            out=o, in0=s, in1=h, op=OP.subtract)))(
                            lo, Ht[par][ncc][:, :], hi),
                            reads=[("Ht", par, ncc), ("Hshi", ncc, s4)],
                            writes=[("Hslo", ncc, s4)])



            emit_A0(0)
            for st in range(2 * NST):
                emit_mm(st)
                if st + 1 < 2 * NST:
                    emit_A0(st + 1)
                emit_post(st)
                emit_prep(prep_slices[st], pbufs)

            prep_ctx.close()
            sc.barrier()

            with contextlib.ExitStack() as m_ctx:
                ME = m_ctx.enter_context
                usum = [ME(nc.sbuf_tensor(f"usum{k}", [P, ST], F32)) for k in range(2)]
                t2b = [ME(nc.sbuf_tensor(f"t2b{k}", [P, ST], F32)) for k in range(2)]
                res_stack = contextlib.ExitStack()
                res1 = res_stack.enter_context(
                    nc.sbuf_tensor("res1", [P, DC * TL], F32, side="right"))

                for tt in range(NST):
                    for dc in range(DC):
                        bk = 6 + dc % 2
                        k = 0
                        for ncc in range(NCC):
                            whi = wcT_hi[:, (ncc * DC + dc) * P:(ncc * DC + dc) * P + P]
                            wlo = wcT_lo[:, (ncc * DC + dc) * P:(ncc * DC + dc) * P + P]
                            for (lhsT, rhs, rt) in [
                                (whi, H_hiF[ncc][:, tt * ST:(tt + 1) * ST], ("Hshi", ncc, tt)),
                                (whi, H_loF[ncc][:, tt * ST:(tt + 1) * ST], ("Hslo", ncc, tt)),
                                (wlo, H_hiF[ncc][:, tt * ST:(tt + 1) * ST], ("Hshi", ncc, tt)),
                            ]:
                                sc.step("tensor", (lambda o, l, r, st_, sp_: (
                                    lambda e: e.matmul(o, l, r, start=st_, stop=sp_)))(
                                    bank(bk), lhsT, rhs, k == 0, k == 5),
                                    reads=[rt, "wcT"], writes=[("pb", bk)], lazy=(k < 5))
                                k += 1
                        uh = u_hiF[:, dc * TL + tt * ST: dc * TL + (tt + 1) * ST]
                        ul = u_loF[:, dc * TL + tt * ST: dc * TL + (tt + 1) * ST]
                        sc.step("vector", (lambda o, a, b2: (lambda e: e.tensor_tensor(
                            out=o, in0=a, in1=b2, op=OP.add)))(usum[dc % 2][:, :], uh, ul),
                            reads=[("uhit", tt), ("ulot", tt)], writes=[("usum", dc % 2)])
                        sc.step("scalar", (lambda o, s, scl: (lambda e: e.activation(
                            out=o, in_=s, func=AF.Identity, bias=0.0, scale=scl)))(
                            t2b[dc % 2][:, :], usum[dc % 2][:, :], dp1_sb[:, dc:dc + 1]),
                            reads=[("usum", dc % 2), "dp1"], writes=[("t2b", dc % 2)])
                        sc.step("vector", (lambda o, a, b2: (lambda e: e.tensor_tensor(
                            out=o, in0=a, in1=b2, op=OP.add)))(
                            res1[:, dc * TL + tt * ST: dc * TL + (tt + 1) * ST],
                            t2b[dc % 2][:, :], bank(bk)),
                            reads=[("t2b", dc % 2), ("pb", bk)], writes=[("res1", dc, tt)])
        sc.barrier()

        lnb["x1tok"] = [E(nc.sbuf_tensor(f"x1tok{k}", [P, D], F32)) for k in range(4)]
        lnb["stat6"] = [E(nc.sbuf_tensor(f"stat6_{k}", [P, 12], F32)) for k in range(2)]
        lnb["mv"] = [E(nc.sbuf_tensor(f"mv{k}", [P, 2], F32)) for k in range(2)]
        lnb["sdv"] = [E(nc.sbuf_tensor(f"sdv{k}", [P, 1], F32)) for k in range(2)]
        lnb["rstd"] = [E(nc.sbuf_tensor(f"rstd{k}", [P, 1], F32)) for k in range(2)]
        lnb["nmr"] = [E(nc.sbuf_tensor(f"nmr{k}", [P, 1], F32)) for k in range(2)]
        x1tok = lnb["x1tok"]
        x1hi = E(nc.sbuf_tensor("x1hi", [P, DC * TL], BF16))
        x1lo = E(nc.sbuf_tensor("x1lo", [P, DC * TL], BF16))

        for s4 in range(NST):
            for g in range(4):
                gg = s4 * 4 + g
                ln_group(lambda dc, gg=gg: res1[:, dc * TL + gg * P: dc * TL + (gg + 1) * P],
                         g, lambda dc, s4=s4: [("res1", dc, s4)])
            for dc in range(DC):
                for g in range(4):
                    sc.step("tensor", (lambda o, sp: (lambda e: e.transpose(o, sp, id32[:, :])))(
                        bank(dc)[:, g * P:(g + 1) * P],
                        x1tok[g][:, dc * P:(dc + 1) * P]),
                        reads=[("x1tokbuf", g), "id32"], writes=[("pb", dc)],
                        lazy=(g < 3))
                hi = x1hi[:, dc * TL + s4 * ST: dc * TL + (s4 + 1) * ST]
                lo = x1lo[:, dc * TL + s4 * ST: dc * TL + (s4 + 1) * ST]
                sc.step("scalar", (lambda o, sp: (lambda e: e.copy(out=o, in_=sp)))(
                    hi, bank(dc)),
                    reads=[("pb", dc)], writes=[("x1hi", dc, s4)])
                sc.step("vector", (lambda o, sp, h: (lambda e: e.tensor_tensor(
                    out=o, in0=sp, in1=h, op=OP.subtract)))(lo, bank(dc), hi),
                    reads=[("pb", dc), ("x1hi", dc, s4)], writes=[("x1lo", dc, s4)])

        res_stack.close()
        sc.barrier()
        prep_fence = sc.fence(["qps0", "qps1", "qpw0", "qpw1"])

        wbufg = [E(nc.sbuf_tensor(f"wbufg{k}", [P, D], BF16)) for k in range(2)]
        wbufu = [E(nc.sbuf_tensor(f"wbufu{k}", [P, D], BF16)) for k in range(2)]
        wbufd = [E(nc.sbuf_tensor(f"wbufd{k}", [P, D], BF16)) for k in range(2)]
        g_sb = [E(nc.sbuf_tensor(f"g_sb{k}", [P, ST], F32)) for k in range(2)]
        hhf = [E(nc.sbuf_tensor(f"hhf{k}", [P, ST], F32)) for k in range(2)]
        x1s = [E(nc.sbuf_tensor(f"x1s{k}", [P, ST], F32)) for k in range(2)]
        res2 = E(nc.sbuf_tensor("res2", [P, DC * ST], F32))
        hh_hi = E(nc.sbuf_tensor("hh_hi", [P, HC * ST], BF16, side="right"))
        hh_lo = E(nc.sbuf_tensor("hh_lo", [P, HC * ST], BF16, side="right"))

        GB = [0, 1, 2]
        UB = [3, 4, 5]

        for m in range(NST):
            for hc in range(HC):
                sl = hc % 2
                ed = prep_fence if (m == 0 and hc < 2) else ()
                sc.step("sync", (lambda d, s: (lambda e: e.dma_start(out=d, in_=s)))(
                    wbufg[sl][:, :], wgT_s[hc, :, :]),
                    writes=[("wbufg", sl)], extra_deps=ed, sem=f"qg{sl}", n=16)
                sc.step("sync", (lambda d, s: (lambda e: e.dma_start(out=d, in_=s)))(
                    wbufu[sl][:, :], wuT_s[hc, :, :]),
                    writes=[("wbufu", sl)], extra_deps=ed, sem=f"qq{sl}", n=16)
                bg, bu_ = GB[hc % 3], UB[hc % 3]
                for (wb2, bb, tok) in [(wbufg[sl], bg, ("wbufg", sl)),
                                       (wbufu[sl], bu_, ("wbufu", sl))]:
                    k = 0
                    for dc in range(DC):
                        for (rhs, rt) in [
                            (x1hi[:, dc * TL + m * ST: dc * TL + (m + 1) * ST],
                             ("x1hi", dc, m)),
                            (x1lo[:, dc * TL + m * ST: dc * TL + (m + 1) * ST],
                             ("x1lo", dc, m)),
                        ]:
                            sc.step("tensor", (lambda o, l, r, st_, sp_: (
                                lambda e: e.matmul(o, l, r, start=st_, stop=sp_)))(
                                bank(bb), wb2[:, dc * P:(dc + 1) * P], rhs,
                                k == 0, k == 2 * DC - 1),
                                reads=[rt, tok], writes=[("pb", bb)],
                                lazy=(k < 2 * DC - 1))
                            k += 1
                sc.step("scalar", (lambda o, s: (lambda e: e.activation(
                    out=o, in_=s, func=AF.Silu, bias=0.0, scale=SCALE)))(
                    g_sb[sl][:, :], bank(bg)),
                    reads=[("pb", bg)], writes=[("g_sb", sl)])
                sc.step("vector", (lambda o, a, b2: (lambda e: e.tensor_tensor(
                    out=o, in0=a, in1=b2, op=OP.mult)))(
                    hhf[sl][:, :], g_sb[sl][:, :], bank(bu_)),
                    reads=[("g_sb", sl), ("pb", bu_)], writes=[("hhf", sl)])
                hi = hh_hi[:, hc * ST:(hc + 1) * ST]
                lo = hh_lo[:, hc * ST:(hc + 1) * ST]
                sc.step("scalar", (lambda o, s: (lambda e: e.mul(out=o, in_=s, mul=SCALE)))(
                    hi, hhf[sl][:, :]),
                    reads=[("hhf", sl)], writes=[("hhhi", hc)])
                sc.step("vector", (lambda o, s, h: (lambda e: e.scalar_tensor_tensor(
                    out=o, in0=s, scalar=SCALE, in1=h, op0=OP.mult, op1=OP.subtract)))(
                    lo, hhf[sl][:, :], hi),
                    reads=[("hhf", sl), ("hhhi", hc)], writes=[("hhlo", hc)])

            for hc in range(HC):
                sl = hc % 2
                sc.step("gpsimd", (lambda d, s: (lambda e: e.dma_start(out=d, in_=s)))(
                    wbufd[sl][:, :], wdT_s[hc, :, :]),
                    writes=[("wbufd", sl)],
                    extra_deps=prep_fence if (m == 0 and hc == 0) else (),
                    sem=f"qd{sl}", n=16)
                for dc in range(DC):
                    for pi, (rhs, rt) in enumerate([
                        (hh_hi[:, hc * ST:(hc + 1) * ST], ("hhhi", hc)),
                        (hh_lo[:, hc * ST:(hc + 1) * ST], ("hhlo", hc)),
                    ]):
                        k = hc * 2 + pi
                        sc.step("tensor", (lambda o, l, r, st_, sp_: (
                            lambda e: e.matmul(o, l, r, start=st_, stop=sp_)))(
                            bank(dc), wbufd[sl][:, dc * P:(dc + 1) * P], rhs,
                            k == 0, k == 2 * HC - 1),
                            reads=[rt, ("wbufd", sl)], writes=[("pb", dc)],
                            lazy=not (dc == DC - 1 and pi == 1))

            for dc in range(DC):
                sl = dc % 2
                uh = x1hi[:, dc * TL + m * ST: dc * TL + (m + 1) * ST]
                ul = x1lo[:, dc * TL + m * ST: dc * TL + (m + 1) * ST]
                sc.step("vector", (lambda o, a, b2: (lambda e: e.tensor_tensor(
                    out=o, in0=a, in1=b2, op=OP.add)))(x1s[sl][:, :], uh, ul),
                    reads=[("x1hi", dc, m), ("x1lo", dc, m)], writes=[("x1s", sl)])
                sc.step("vector", (lambda o, s, a: (lambda e: e.scalar_tensor_tensor(
                    out=o, in0=s, scalar=SCALE, in1=a, op0=OP.mult, op1=OP.add)))(
                    res2[:, dc * ST:(dc + 1) * ST], bank(dc), x1s[sl][:, :]),
                    reads=[("pb", dc), ("x1s", sl)], writes=[("res2", dc)])

            for g in range(4):
                gg = m * 4 + g
                sl = g % 2
                ln_group(lambda dc, g=g: res2[:, dc * ST + g * P: dc * ST + (g + 1) * P],
                         sl, lambda dc: [("res2", dc)])
                sc.step("gpsimd", (lambda d, s: (lambda e: e.dma_start(out=d, in_=s)))(
                    out_h[gg * P:(gg + 1) * P, :], x1tok[sl][:, :]),
                    reads=[("x1tokbuf", sl)], sem=f"qo{sl}", n=16)

        qo_tot = (sc.seq.get("qo0", 0), sc.seq.get("qo1", 0))

        @block.tensor
        def _(eng):
            sc.emit("tensor", eng, sems)

        @block.scalar
        def _(eng):
            sc.emit("scalar", eng, sems)

        @block.vector
        def _(eng):
            sc.emit("vector", eng, sems)

        @block.gpsimd
        def _(eng):
            sc.emit("gpsimd", eng, sems)
            eng.wait_ge(sems["qo0"], qo_tot[0])
            eng.wait_ge(sems["qo1"], qo_tot[1])

        @block.sync
        def _(eng):
            sc.emit("sync", eng, sems)
            eng.wait_ge(sems["qo0"], qo_tot[0])
            eng.wait_ge(sems["qo1"], qo_tot[1])

    return nc


_NC_CACHE = [None]


def kernel(L0_attn_u, L0_attn_Wl, L0_attn_bl, L0_attn_Wb, L0_attn_Wc, L0_attn_Dp,
           L0_ffn_w_gate, L0_ffn_w_up, L0_ffn_w_down):
    u = np.ascontiguousarray(np.asarray(L0_attn_u, dtype=np.float32))
    shared = dict(
        wl=np.ascontiguousarray(np.asarray(L0_attn_Wl, np.float32)),
        bl=np.ascontiguousarray(np.asarray(L0_attn_bl, np.float32)),
        wb=np.ascontiguousarray(np.asarray(L0_attn_Wb, np.float32)),
        wc=np.ascontiguousarray(np.asarray(L0_attn_Wc, np.float32)),
        dp=np.ascontiguousarray(np.asarray(L0_attn_Dp, np.float32)),
        wg=np.ascontiguousarray(np.asarray(L0_ffn_w_gate, np.int32)),
        wu=np.ascontiguousarray(np.asarray(L0_ffn_w_up, np.int32)),
        wd=np.ascontiguousarray(np.asarray(L0_ffn_w_down, np.int32)),
    )
    zeros = np.zeros((TL, D), np.float32)
    in_maps = []
    for c in range(8):
        b, h = divmod(c, 2)
        mmap = dict(shared)
        mmap["u_main"] = np.ascontiguousarray(u[b, h * TL:(h + 1) * TL])
        mmap["u_pre"] = np.ascontiguousarray(u[b, 0:TL]) if h == 1 else zeros
        in_maps.append(mmap)

    if _NC_CACHE[0] is None:
        _NC_CACHE[0] = build_nc()
    nc = _NC_CACHE[0]
    res = run_bass_kernel_spmd(nc, in_maps, core_ids=list(range(8)))

    out = np.empty((B, T, D), np.float32)
    for c in range(8):
        b, h = divmod(c, 2)
        out[b, h * TL:(h + 1) * TL] = res.results[c]["out"]
    return out


# version 14
# speedup vs baseline: 1.1925x; 1.0091x over previous
import contextlib
import numpy as np

import concourse.bass as bass
import concourse.mybir as mybir
from concourse.bass_utils import run_bass_kernel_spmd

F32 = mybir.dt.float32
BF16 = mybir.dt.bfloat16
I32 = mybir.dt.int32
AF = mybir.ActivationFunctionType
OP = mybir.AluOpType

P = 128
B, T, D, N, HID = 4, 4096, 1024, 256, 3072
TL = T // 2
ST = 512
NST = TL // ST
DC = D // P
NCC = N // P
HC = HID // P
SCALE = float(np.float32(1.0 / 127.0))
EPS = 1e-5


class Sched:

    def __init__(self):
        self.steps = []
        self.seq = {}
        self.last_w = {}
        self.readers = {}
        self.emitted_wait = {}
        self.dma_sems = set()
        self.pending = {}

    def step(self, sec, fn, reads=(), writes=(), extra_deps=(), sem=None,
             n=1, lazy=False):
        if sem is None:
            sem = sec
        pf = getattr(self, "pending_fence", None)
        if pf and sec in pf:
            extra_deps = list(extra_deps) + pf.pop(sec)
        if n == 16:
            self.dma_sems.add(sem)
        deps = {}

        def add(dep):
            if dep is None:
                return
            e, v = dep
            if v == "PENDING":
                assert e == sem, (
                    f"dependency on a lazy step's output via sem {e} from a "
                    f"step on sem {sem}; make that producer non-lazy")
                return
            if deps.get(e, 0) < v:
                deps[e] = v

        for t in reads:
            add(self.last_w.get(t))
        for t in writes:
            add(self.last_w.get(t))
            for r in self.readers.get(t, ()):
                add(r)
        for d in extra_deps:
            add(d)

        if sec == "tensor":
            deps.pop("tensor", None)

        for (e, v) in deps.items():
            if e in self.dma_sems:
                assert v == self.seq.get(e, 0), (
                    f"stale DMA lane wait: need {e}>={v}, lane at "
                    f"{self.seq.get(e, 0)}")

        waits = []
        for (e, v) in sorted(deps.items(), key=str):
            key = (sec, e)
            if self.emitted_wait.get(key, 0) >= v:
                continue
            self.emitted_wait[key] = v
            waits.append((e, v))

        if lazy:
            assert sec == "tensor", "lazy steps are tensor-only"
            tok = (sem, "PENDING")
            self.steps.append((sec, waits, fn, None, 0))
            pend = self.pending.setdefault(sem, [])
            for t in reads:
                pend.append(("r", t))
            for t in writes:
                self.last_w[t] = tok
                self.readers[t] = []
                pend.append(("w", t))
            return tok

        val_after = self.seq.get(sem, 0) + n
        self.seq[sem] = val_after
        self.steps.append((sec, waits, fn, sem, n))
        tok = (sem, val_after)
        for kind, t in self.pending.pop(sem, ()):
            if kind == "w":
                if self.last_w.get(t, (None, None))[1] == "PENDING":
                    self.last_w[t] = tok
            else:
                self.readers.setdefault(t, []).append(tok)
        for t in reads:
            self.readers.setdefault(t, []).append(tok)
        for t in writes:
            self.last_w[t] = tok
            self.readers[t] = []
        return tok

    def fence(self, sems_):
        return [(s, self.seq.get(s, 0)) for s in sems_]

    def barrier(self, exclude=()):
        f = self.fence([k for k in self.seq.keys() if k not in exclude])
        self.pending_fence = {s: list(f) for s in
                              ("tensor", "scalar", "vector", "gpsimd", "sync")}

    def emit(self, name, eng_obj, semmap):
        for (sec, waits, fn, sem, n) in self.steps:
            if sec != name:
                continue
            for (s, v) in waits:
                eng_obj.wait_ge(semmap[s], v)
            inst = fn(eng_obj)
            if sem is not None:
                inst.then_inc(semmap[sem], n)


def build_nc():
    nc = bass.Bass("TRN2", target_bir_lowering=False)

    u_main = nc.dram_tensor("u_main", [TL, D], F32, kind="ExternalInput").ap()
    u_pre = nc.dram_tensor("u_pre", [TL, D], F32, kind="ExternalInput").ap()
    wl_h = nc.dram_tensor("wl", [N, D], F32, kind="ExternalInput").ap()
    bl_h = nc.dram_tensor("bl", [N], F32, kind="ExternalInput").ap()
    wb_h = nc.dram_tensor("wb", [N, D], F32, kind="ExternalInput").ap()
    wc_h = nc.dram_tensor("wc", [D, N], F32, kind="ExternalInput").ap()
    dp_h = nc.dram_tensor("dp", [D], F32, kind="ExternalInput").ap()
    wg_h = nc.dram_tensor("wg", [HID, D], I32, kind="ExternalInput").ap()
    wu_h = nc.dram_tensor("wu", [HID, D], I32, kind="ExternalInput").ap()
    wd_h = nc.dram_tensor("wd", [D, HID], I32, kind="ExternalInput").ap()
    out_h = nc.dram_tensor("out", [TL, D], F32, kind="ExternalOutput").ap()

    wgT_s = nc.dram_tensor("wgT_s", [HC, P, D], BF16).ap()
    wuT_s = nc.dram_tensor("wuT_s", [HC, P, D], BF16).ap()
    wdT_s = nc.dram_tensor("wdT_s", [HC, P, D], BF16).ap()

    sc = Sched()
    sem_names = ["tensor", "scalar", "vector", "gpsimd",
                 "qc0", "qc1", "qu0", "qu1", "qp0", "qp1", "qps0", "qps1",
                 "qw0", "qw1", "qw2", "qw3", "qx0", "qx1", "qpw0", "qpw1",
                 "qg0", "qg1", "qq0", "qq1", "qd0", "qd1", "qo0", "qo1"]

    with contextlib.ExitStack() as ctx:
        E = ctx.enter_context

        id32 = E(nc.sbuf_tensor("id32", [P, P], F32))
        idbf = E(nc.sbuf_tensor("idbf", [P, P], BF16))
        eps_sb = E(nc.sbuf_tensor("eps_sb", [P, 1], F32))
        bl_sb = E(nc.sbuf_tensor("bl_sb", [P, NCC], F32))
        dp1_sb = E(nc.sbuf_tensor("dp1_sb", [P, DC], F32))
        wlbT_hi = E(nc.sbuf_tensor("wlbT_hi", [P, 4 * D], BF16))
        wlbT_lo = E(nc.sbuf_tensor("wlbT_lo", [P, 4 * D], BF16))
        wcT_hi = E(nc.sbuf_tensor("wcT_hi", [P, NCC * D], BF16))
        wcT_lo = E(nc.sbuf_tensor("wcT_lo", [P, NCC * D], BF16))

        pp = [E(nc.psum_tensor(f"pp{i}", [P, 1024], F32)) for i in range(4)]
        ps = [pp[0][:, 0:512], pp[0][:, 512:1024],
              pp[1][:, 0:512], pp[1][:, 512:1024]]
        pt = [pp[2], pp[3]]

        def bank(i):
            if i < 4:
                return ps[i]
            j, half = divmod(i - 4, 2)
            return pt[j][:, half * 512:(half + 1) * 512]

        sems = {s: E(nc.semaphore(s)) for s in sem_names}
        block = E(nc.Block())

        lnb = {}

        sc.step("gpsimd", lambda e: e.memset(id32[:, :], 0.0), writes=["id32"])
        sc.step("gpsimd", lambda e: e.affine_select(
            out=id32[:, :], in_=id32[:, :], compare_op=OP.not_equal, fill=1.0,
            base=0, pattern=[[-1, P]], channel_multiplier=1),
            reads=["id32"], writes=["id32"])
        sc.step("gpsimd", lambda e: e.memset(idbf[:, :], 0.0), writes=["idbf"])
        sc.step("gpsimd", lambda e: e.affine_select(
            out=idbf[:, :], in_=idbf[:, :], compare_op=OP.not_equal, fill=1.0,
            base=0, pattern=[[-1, P]], channel_multiplier=1),
            reads=["idbf"], writes=["idbf"])
        sc.step("gpsimd", lambda e: e.memset(eps_sb[:, :], EPS), writes=["eps"])
        sc.step("sync", lambda e: e.dma_start(
            out=bl_sb[:, :], in_=bl_h.rearrange("(c p) -> p c", p=P),
            allow_slow_non_contiguous=True),
            writes=["bl"], sem="qc0", n=16)
        sc.step("sync", lambda e: e.dma_start(
            out=dp1_sb[:, :], in_=dp_h.rearrange("(c p) -> p c", p=P),
            allow_slow_non_contiguous=True),
            writes=["dp1"], sem="qc1", n=16)
        sc.step("scalar", lambda e: e.add(out=dp1_sb[:, :], in_=dp1_sb[:, :], add=1.0),
                reads=["dp1"], writes=["dp1"])

        def ln_group(src_ap_fn, gpar, rd_toks, pair=None):
            sl = gpar % 2
            ptk = pt[sl if pair is None else pair]
            x1tok, stat6, mv = lnb["x1tok"], lnb["stat6"], lnb["mv"]
            sdv, rstd, nmr = lnb["sdv"], lnb["rstd"], lnb["nmr"]
            pr = sl if pair is None else pair
            btoks = [("pb", 4 + 2 * pr), ("pb", 5 + 2 * pr)]
            for dc in range(DC):
                sc.step("tensor", (lambda o, s: (lambda e: e.transpose(o, s, id32[:, :])))(
                    ptk[:, dc * P:(dc + 1) * P], src_ap_fn(dc)),
                    reads=rd_toks(dc) + ["id32"],
                    writes=[btoks[dc // 4]], lazy=(dc < DC - 1))
            for half in range(2):
                sc.step("vector", (lambda o, s: (lambda e: e.bn_stats(out=o, in_=s)))(
                    stat6[sl][:, half * 6:(half + 1) * 6],
                    ptk[:, half * 512:(half + 1) * 512]),
                    reads=[btoks[half]], writes=[("stat", sl)])
            sc.step("vector", (lambda o, s: (lambda e: e.bn_aggr(out=o, in_=s)))(
                mv[sl][:, :], stat6[sl][:, :]),
                reads=[("stat", sl)], writes=[("mv", sl)])
            sc.step("scalar", (lambda o, s: (lambda e: e.activation(
                out=o, in_=s, func=AF.Sqrt, bias=eps_sb[:, 0:1], scale=1.0)))(
                sdv[sl][:, :], mv[sl][:, 1:2]),
                reads=[("mv", sl), "eps"], writes=[("sd", sl)])
            sc.step("vector", (lambda o, s: (lambda e: e.reciprocal(out=o, in_=s)))(
                rstd[sl][:, :], sdv[sl][:, :]),
                reads=[("sd", sl)], writes=[("rstd", sl)])
            sc.step("vector", (lambda o, a, b2: (lambda e: e.tensor_tensor(
                out=o, in0=a, in1=b2, op=OP.mult)))(
                nmr[sl][:, :], mv[sl][:, 0:1], rstd[sl][:, :]),
                reads=[("mv", sl), ("rstd", sl)], writes=[("nmr", sl)])
            sc.step("scalar", (lambda o, s: (lambda e: e.mul(out=o, in_=s, mul=-1.0)))(
                nmr[sl][:, :], nmr[sl][:, :]),
                reads=[("nmr", sl)], writes=[("nmr", sl)])
            sc.step("scalar", (lambda o, s, r, b2: (lambda e: e.activation(
                out=o, in_=s, func=AF.Identity, bias=b2, scale=r)))(
                x1tok[gpar][:, :], ptk[:, :], rstd[sl][:, :], nmr[sl][:, :]),
                reads=btoks + [("rstd", sl), ("nmr", sl)],
                writes=[("x1tokbuf", gpar)])

        gu_items = []
        for (w_src, w_dst) in [(wg_h, wgT_s), (wu_h, wuT_s)]:
            for hc4 in range(HC // 4):
                gu_items.append(("gu4", w_src, w_dst, hc4 * 4))
        wd_items = [("wd", dc) for dc in range(DC)]
        prep_items = []
        for i in range(NST):
            prep_items.append(wd_items[2 * i])
            prep_items.extend(gu_items[3 * i:3 * i + 2])
            prep_items.append(wd_items[2 * i + 1])
            prep_items.append(gu_items[3 * i + 2])

        def emit_prep(items, bufs):
            cv_gu, cv_wd, stg = bufs["cv_gu"], bufs["cv_wd"], bufs["stg"]
            for it in items:
                if it[0] == "gu4":
                    _, w_src, w_dst, hc0 = it
                    sl = (hc0 // 4) % 2
                    sc.step("gpsimd", (lambda d, s: (lambda e: e.dma_start(out=d, in_=s)))(
                        cv_gu[sl].rearrange("p (c j) -> p c j", j=D),
                        w_src[hc0 * P:(hc0 + 4) * P, :].rearrange(
                            "(c p) j -> p c j", p=P)),
                        writes=[("cvgu", sl)], sem=f"qp{sl}", n=16)
                    for q in range(4):
                        hc = hc0 + q
                        for dc in range(DC):
                            sc.step("tensor", (lambda o, s: (lambda e: e.transpose(o, s, idbf[:, :])))(
                                ps[sl][:, dc * 64:(dc + 1) * 64].bitcast(BF16),
                                cv_gu[sl][:, (q * DC + dc) * P:(q * DC + dc) * P + P]),
                                reads=[("cvgu", sl), "idbf"], writes=[("pb", sl)],
                                lazy=(dc < DC - 1))
                        sc.step("scalar", (lambda o, s: (lambda e: e.copy(out=o, in_=s)))(
                            stg[sl][:, q * D:(q + 1) * D],
                            ps[sl][:, 0:512].bitcast(BF16)),
                            reads=[("pb", sl)], writes=[("stg", sl, q)])
                    sc.step("scalar", (lambda d, s: (lambda e: e.dma_start(out=d, in_=s)))(
                        w_dst[hc0:hc0 + 4, :, :].rearrange("h p j -> p h j"),
                        stg[sl].rearrange("p (h j) -> p h j", h=4)),
                        reads=[("stg", sl, q2) for q2 in range(4)],
                        sem=f"qps{sl}", n=16)
                else:
                    _, dc = it
                    sl = dc % 2
                    sc.step("gpsimd", (lambda d, s: (lambda e: e.dma_start(out=d, in_=s)))(
                        cv_wd[sl][:, :], wd_h[dc * P:(dc + 1) * P, :]),
                        writes=[("cvwd", sl)], sem=f"qx{sl}", n=16)
                    stgw = bufs["stgw"]
                    for basehc in range(0, HC, 8):
                        for k in range(8):
                            hc = basehc + k
                            sc.step("tensor", (lambda o, s: (lambda e: e.transpose(o, s, idbf[:, :])))(
                                ps[2 + sl][:, k * 64:(k + 1) * 64].bitcast(BF16),
                                cv_wd[sl][:, hc * P:(hc + 1) * P]),
                                reads=[("cvwd", sl), "idbf"], writes=[("pb", 2 + sl)],
                                lazy=(k < 7))
                        ssl = (dc + basehc // 8) % 2
                        sc.step("scalar", (lambda o, s: (lambda e: e.copy(out=o, in_=s)))(
                            stgw[ssl][:, :], ps[2 + sl][:, 0:512].bitcast(BF16)),
                            reads=[("pb", 2 + sl)], writes=[("stgw", ssl)])
                        sc.step("scalar", (lambda d, s: (lambda e: e.dma_start(out=d, in_=s)))(
                            wdT_s[basehc:basehc + 8, :, dc * P:(dc + 1) * P].rearrange(
                                "h p j -> p h j"),
                            stgw[ssl][:, :].rearrange("p (h j) -> p h j", h=8)),
                            reads=[("stgw", ssl)], sem=f"qpw{ssl}", n=16)

        with contextlib.ExitStack() as ssm_ctx:
            SE = ssm_ctx.enter_context
            ssm_wlb = SE(nc.sbuf_tensor("ssm_wlb", [P, 4 * D], F32))
            ssm_wc = SE(nc.sbuf_tensor("ssm_wc", [P, NCC * D], F32))
            sc.step("sync", lambda e: e.dma_start(
                out=ssm_wlb[:, 0:2 * D], in_=wl_h.rearrange("(c p) d -> p c d", p=P)),
                writes=["wlbn"], sem="qw1", n=16)
            sc.step("sync", lambda e: e.dma_start(
                out=ssm_wlb[:, 2 * D:4 * D], in_=wb_h.rearrange("(c p) d -> p c d", p=P)),
                writes=["wlbn"], sem="qw2", n=16)
            sc.step("sync", lambda e: e.dma_start(
                out=ssm_wc[:, :], in_=wc_h.rearrange("(c p) n -> p c n", p=P)),
                writes=["wcn"], sem="qw3", n=16)

            tiles = [("wlb", i, dc) for i in range(4) for dc in range(DC)] + \
                    [("wc", ncc, dc) for ncc in range(NCC) for dc in range(DC)]
            for base in range(0, len(tiles), 4):
                grp = tiles[base:base + 4]
                bidx = (base // 4) % 2
                for k, (kind, a, dc) in enumerate(grp):
                    if kind == "wlb":
                        srcap = ssm_wlb[:, (a * DC + dc) * P:(a * DC + dc) * P + P]
                        rtok = "wlbn"
                    else:
                        srcap = ssm_wc[:, dc * N + a * P: dc * N + (a + 1) * P]
                        rtok = "wcn"
                    sc.step("tensor", (lambda o, s: (lambda e: e.transpose(o, s, id32[:, :])))(
                        bank(bidx)[:, k * P:(k + 1) * P], srcap),
                        reads=[rtok, "id32"], writes=[("pb", bidx)],
                        lazy=(k < len(grp) - 1))
                for k, (kind, a, dc) in enumerate(grp):
                    if kind == "wlb":
                        hi = wlbT_hi[:, (a * DC + dc) * P:(a * DC + dc) * P + P]
                        lo = wlbT_lo[:, (a * DC + dc) * P:(a * DC + dc) * P + P]
                        wtok = "wlbT"
                    else:
                        hi = wcT_hi[:, (a * DC + dc) * P:(a * DC + dc) * P + P]
                        lo = wcT_lo[:, (a * DC + dc) * P:(a * DC + dc) * P + P]
                        wtok = "wcT"
                    pbk = bank(bidx)[:, k * P:(k + 1) * P]
                    sc.step("scalar", (lambda o, s: (lambda e: e.copy(out=o, in_=s)))(hi, pbk),
                            reads=[("pb", bidx)], writes=[wtok])
                    sc.step("vector", (lambda o, s, h: (lambda e: e.tensor_tensor(
                        out=o, in0=s, in1=h, op=OP.subtract)))(lo, pbk, hi),
                        reads=[("pb", bidx), wtok], writes=[wtok])
        sc.barrier()

        with contextlib.ExitStack() as a_ctx:
            AE = a_ctx.enter_context
            u_hiF = AE(nc.sbuf_tensor("u_hiF", [P, DC * TL], BF16))
            u_loF = AE(nc.sbuf_tensor("u_loF", [P, DC * TL], BF16))
            H_hiF = [AE(nc.sbuf_tensor(f"H_hiF{k}", [P, TL], BF16)) for k in range(NCC)]
            H_loF = [AE(nc.sbuf_tensor(f"H_loF{k}", [P, TL], BF16)) for k in range(NCC)]
            l1_ctx = contextlib.ExitStack()
            LE = l1_ctx.enter_context
            utok = [LE(nc.sbuf_tensor(f"utok{k}", [P, D], F32)) for k in range(2)]
            lam_sb = [LE(nc.sbuf_tensor(f"lam{k}", [P, 2 * ST], F32)) for k in range(NCC)]
            Ht = [LE(nc.sbuf_tensor(f"Ht{k}", [P, 2 * ST], F32)) for k in range(NCC)]
            carry = LE(nc.sbuf_tensor("carry", [P, NCC], F32))

            prep_ctx = contextlib.ExitStack()
            PE2 = prep_ctx.enter_context
            pbufs = dict(
                cv_gu=[PE2(nc.sbuf_tensor(f"cvgu{k}", [P, 4 * D], BF16)) for k in range(2)],
                cv_wd=[PE2(nc.sbuf_tensor(f"cvwd{k}", [P, HID], BF16)) for k in range(2)],
                stg=[PE2(nc.sbuf_tensor(f"stg{k}", [P, 4 * D], BF16)) for k in range(2)],
                stgw=[PE2(nc.sbuf_tensor(f"stgw{k}", [P, D], BF16)) for k in range(2)],
            )

            nprep = len(prep_items)
            NSL = 2 * NST
            prep_slices = [prep_items[(nprep * i) // NSL:(nprep * (i + 1)) // NSL]
                           for i in range(NSL)]

            SS = 2 * ST
            NS1 = 2 * TL // SS

            def emit_A0(st):
                pre = st < NS1 // 2
                s2 = st % (NS1 // 2)
                src = u_pre if pre else u_main
                for g in range(8):
                    sl = g % 2
                    ppA = pp[sl]
                    rows = src[s2 * SS + g * P: s2 * SS + (g + 1) * P, :]
                    sc.step("sync", (lambda d, sp: (lambda e: e.dma_start(out=d, in_=sp)))(
                        utok[sl][:, :], rows),
                        writes=[("utok", sl)], sem=f"qu{sl}", n=16)
                    for dc in range(DC):
                        sc.step("tensor", (lambda o, sp: (lambda e: e.transpose(o, sp, id32[:, :])))(
                            ppA[:, dc * P:(dc + 1) * P],
                            utok[sl][:, dc * P:(dc + 1) * P]),
                            reads=[("utok", sl), "id32"],
                            writes=[("pb", 2 * sl), ("pb", 2 * sl + 1)],
                            lazy=(dc < DC - 1))
                    coff = s2 * SS + g * P
                    hidst = u_hiF.rearrange("p (c j) -> p c j", j=TL)[
                        :, 0:DC, coff:coff + P]
                    lodst = u_loF.rearrange("p (c j) -> p c j", j=TL)[
                        :, 0:DC, coff:coff + P]
                    srcap = ppA[:, :].rearrange("p (c j) -> p c j", j=P)
                    sc.step("scalar", (lambda o, sp: (lambda e: e.copy(out=o, in_=sp)))(
                        hidst, srcap),
                        reads=[("pb", 2 * sl), ("pb", 2 * sl + 1)], writes=[("uhit", s2)])
                    sc.step("vector", (lambda o, sp, h: (lambda e: e.tensor_tensor(
                        out=o, in0=sp, in1=h, op=OP.subtract)))(
                        lodst, srcap, hidst),
                        reads=[("pb", 2 * sl), ("pb", 2 * sl + 1), ("uhit", s2)],
                        writes=[("ulot", s2)])

            def emit_mmlam(st):
                s2 = st % (NS1 // 2)
                for ncc in range(NCC):
                    ppX = pt[ncc]
                    for half in range(2):
                        nmm = DC * 3
                        k = 0
                        for dc in range(DC):
                            cbase = dc * TL + s2 * SS + half * ST
                            rhs_hi = u_hiF[:, cbase:cbase + ST]
                            rhs_lo = u_loF[:, cbase:cbase + ST]
                            whi = wlbT_hi[:, (ncc * DC + dc) * P:(ncc * DC + dc) * P + P]
                            wlo = wlbT_lo[:, (ncc * DC + dc) * P:(ncc * DC + dc) * P + P]
                            for (lhsT, rhs) in [(whi, rhs_hi), (whi, rhs_lo), (wlo, rhs_hi)]:
                                sc.step("tensor", (lambda o, l, r, st_, sp_: (
                                    lambda e: e.matmul(o, l, r, start=st_, stop=sp_)))(
                                    ppX[:, half * ST:(half + 1) * ST], lhsT, rhs,
                                    k == 0, k == nmm - 1),
                                    reads=[("uhit", s2), ("ulot", s2), "wlbT"],
                                    writes=[("pb", 4 + 2 * ncc + half)],
                                    lazy=(k < nmm - 1))
                                k += 1

            def emit_mmbu(st, ncc):
                s2 = st % (NS1 // 2)
                ppX = pt[ncc]
                for half in range(2):
                    nmm = DC * 3
                    k = 0
                    for dc in range(DC):
                        cbase = dc * TL + s2 * SS + half * ST
                        rhs_hi = u_hiF[:, cbase:cbase + ST]
                        rhs_lo = u_loF[:, cbase:cbase + ST]
                        whi = wlbT_hi[:, ((2 + ncc) * DC + dc) * P:((2 + ncc) * DC + dc) * P + P]
                        wlo = wlbT_lo[:, ((2 + ncc) * DC + dc) * P:((2 + ncc) * DC + dc) * P + P]
                        for (lhsT, rhs) in [(whi, rhs_hi), (whi, rhs_lo), (wlo, rhs_hi)]:
                            sc.step("tensor", (lambda o, l, r, st_, sp_: (
                                lambda e: e.matmul(o, l, r, start=st_, stop=sp_)))(
                                ppX[:, half * ST:(half + 1) * ST], lhsT, rhs,
                                k == 0, k == nmm - 1),
                                reads=[("uhit", s2), ("ulot", s2), "wlbT"],
                                writes=[("pb", 4 + 2 * ncc + half)],
                                lazy=(k < nmm - 1))
                            k += 1

            def emit_sig(st, ncc):
                sc.step("scalar", (lambda o, sp, b2: (lambda e: e.activation(
                    out=o, in_=sp, func=AF.Sigmoid, bias=b2, scale=1.0)))(
                    lam_sb[ncc][:, :], pt[ncc][:, :], bl_sb[:, ncc:ncc + 1]),
                    reads=[("pb", 4 + 2 * ncc), ("pb", 5 + 2 * ncc), "bl"],
                    writes=[("lam", ncc)])

            def emit_scan(st, ncc):
                pre = st < NS1 // 2
                s2 = st % (NS1 // 2)
                init = 0.0 if st == 0 else carry[:, ncc:ncc + 1]
                rd = [("lam", ncc), ("pb", 4 + 2 * ncc), ("pb", 5 + 2 * ncc)]
                if st > 0:
                    rd.append(("carry", ncc))
                sc.step("vector", (lambda o, d0, d1, ini: (lambda e: e.tensor_tensor_scan(
                    out=o, data0=d0, data1=d1, initial=ini, op0=OP.mult, op1=OP.add)))(
                    Ht[ncc][:, :], lam_sb[ncc][:, :], pt[ncc][:, :], init),
                    reads=rd, writes=[("Ht", ncc)])
                sc.step("vector", (lambda o, sp: (lambda e: e.tensor_copy(out=o, in_=sp)))(
                    carry[:, ncc:ncc + 1], Ht[ncc][:, SS - 1:SS]),
                    reads=[("Ht", ncc)], writes=[("carry", ncc)])
                if not pre:
                    hi = H_hiF[ncc][:, s2 * SS:(s2 + 1) * SS]
                    lo = H_loF[ncc][:, s2 * SS:(s2 + 1) * SS]
                    sc.step("scalar", (lambda o, sp: (lambda e: e.copy(out=o, in_=sp)))(
                        hi, Ht[ncc][:, :]),
                        reads=[("Ht", ncc)], writes=[("Hshi", ncc, s2)])
                    sc.step("vector", (lambda o, sp, h: (lambda e: e.tensor_tensor(
                        out=o, in0=sp, in1=h, op=OP.subtract)))(
                        lo, Ht[ncc][:, :], hi),
                        reads=[("Ht", ncc), ("Hshi", ncc, s2)],
                        writes=[("Hslo", ncc, s2)])

            NSL = NS1
            prep_slices = [prep_items[(len(prep_items) * i) // NSL:
                                      (len(prep_items) * (i + 1)) // NSL]
                           for i in range(NSL)]

            emit_A0(0)
            for st in range(NS1):
                emit_mmlam(st)
                emit_sig(st, 0)
                emit_mmbu(st, 0)
                emit_sig(st, 1)
                emit_scan(st, 0)
                emit_mmbu(st, 1)
                if st + 1 < NS1:
                    emit_A0(st + 1)
                emit_scan(st, 1)
                emit_prep(prep_slices[st], pbufs)

            prep_ctx.close()
            l1_ctx.close()
            sc.barrier()

            with contextlib.ExitStack() as m_ctx:
                ME = m_ctx.enter_context
                usum = [ME(nc.sbuf_tensor(f"usum{k}", [P, ST], F32)) for k in range(2)]
                t2b = [ME(nc.sbuf_tensor(f"t2b{k}", [P, ST], F32)) for k in range(2)]
                res_stack = contextlib.ExitStack()
                res1 = res_stack.enter_context(
                    nc.sbuf_tensor("res1", [P, DC * TL], F32, side="right"))

                for tt in range(NST):
                    for dc in range(DC):
                        bk = 6 + dc % 2
                        k = 0
                        for ncc in range(NCC):
                            whi = wcT_hi[:, (ncc * DC + dc) * P:(ncc * DC + dc) * P + P]
                            wlo = wcT_lo[:, (ncc * DC + dc) * P:(ncc * DC + dc) * P + P]
                            for (lhsT, rhs, rt) in [
                                (whi, H_hiF[ncc][:, tt * ST:(tt + 1) * ST], ("Hshi", ncc, tt)),
                                (whi, H_loF[ncc][:, tt * ST:(tt + 1) * ST], ("Hslo", ncc, tt)),
                                (wlo, H_hiF[ncc][:, tt * ST:(tt + 1) * ST], ("Hshi", ncc, tt)),
                            ]:
                                sc.step("tensor", (lambda o, l, r, st_, sp_: (
                                    lambda e: e.matmul(o, l, r, start=st_, stop=sp_)))(
                                    bank(bk), lhsT, rhs, k == 0, k == 5),
                                    reads=[rt, "wcT"], writes=[("pb", bk)], lazy=(k < 5))
                                k += 1
                        uh = u_hiF[:, dc * TL + tt * ST: dc * TL + (tt + 1) * ST]
                        ul = u_loF[:, dc * TL + tt * ST: dc * TL + (tt + 1) * ST]
                        sc.step("vector", (lambda o, a, b2: (lambda e: e.tensor_tensor(
                            out=o, in0=a, in1=b2, op=OP.add)))(usum[dc % 2][:, :], uh, ul),
                            reads=[("uhit", tt), ("ulot", tt)], writes=[("usum", dc % 2)])
                        sc.step("scalar", (lambda o, s, scl: (lambda e: e.activation(
                            out=o, in_=s, func=AF.Identity, bias=0.0, scale=scl)))(
                            t2b[dc % 2][:, :], usum[dc % 2][:, :], dp1_sb[:, dc:dc + 1]),
                            reads=[("usum", dc % 2), "dp1"], writes=[("t2b", dc % 2)])
                        sc.step("vector", (lambda o, a, b2: (lambda e: e.tensor_tensor(
                            out=o, in0=a, in1=b2, op=OP.add)))(
                            res1[:, dc * TL + tt * ST: dc * TL + (tt + 1) * ST],
                            t2b[dc % 2][:, :], bank(bk)),
                            reads=[("t2b", dc % 2), ("pb", bk)], writes=[("res1", dc, tt)])
        sc.barrier()

        lnb["x1tok"] = [E(nc.sbuf_tensor(f"x1tok{k}", [P, D], F32)) for k in range(4)]
        lnb["stat6"] = [E(nc.sbuf_tensor(f"stat6_{k}", [P, 12], F32)) for k in range(2)]
        lnb["mv"] = [E(nc.sbuf_tensor(f"mv{k}", [P, 2], F32)) for k in range(2)]
        lnb["sdv"] = [E(nc.sbuf_tensor(f"sdv{k}", [P, 1], F32)) for k in range(2)]
        lnb["rstd"] = [E(nc.sbuf_tensor(f"rstd{k}", [P, 1], F32)) for k in range(2)]
        lnb["nmr"] = [E(nc.sbuf_tensor(f"nmr{k}", [P, 1], F32)) for k in range(2)]
        x1tok = lnb["x1tok"]
        x1hi = E(nc.sbuf_tensor("x1hi", [P, DC * TL], BF16))
        x1lo = E(nc.sbuf_tensor("x1lo", [P, DC * TL], BF16))

        for s4 in range(NST):
            for g in range(4):
                gg = s4 * 4 + g
                ln_group(lambda dc, gg=gg: res1[:, dc * TL + gg * P: dc * TL + (gg + 1) * P],
                         g, lambda dc, s4=s4: [("res1", dc, s4)])
            for dc in range(DC):
                for g in range(4):
                    sc.step("tensor", (lambda o, sp: (lambda e: e.transpose(o, sp, id32[:, :])))(
                        bank(dc)[:, g * P:(g + 1) * P],
                        x1tok[g][:, dc * P:(dc + 1) * P]),
                        reads=[("x1tokbuf", g), "id32"], writes=[("pb", dc)],
                        lazy=(g < 3))
                hi = x1hi[:, dc * TL + s4 * ST: dc * TL + (s4 + 1) * ST]
                lo = x1lo[:, dc * TL + s4 * ST: dc * TL + (s4 + 1) * ST]
                sc.step("scalar", (lambda o, sp: (lambda e: e.copy(out=o, in_=sp)))(
                    hi, bank(dc)),
                    reads=[("pb", dc)], writes=[("x1hi", dc, s4)])
                sc.step("vector", (lambda o, sp, h: (lambda e: e.tensor_tensor(
                    out=o, in0=sp, in1=h, op=OP.subtract)))(lo, bank(dc), hi),
                    reads=[("pb", dc), ("x1hi", dc, s4)], writes=[("x1lo", dc, s4)])

        res_stack.close()
        sc.barrier()
        prep_fence = sc.fence(["qps0", "qps1", "qpw0", "qpw1"])

        wbufg = [E(nc.sbuf_tensor(f"wbufg{k}", [P, D], BF16)) for k in range(2)]
        wbufu = [E(nc.sbuf_tensor(f"wbufu{k}", [P, D], BF16)) for k in range(2)]
        wbufd = [E(nc.sbuf_tensor(f"wbufd{k}", [P, D], BF16)) for k in range(2)]
        g_sb = [E(nc.sbuf_tensor(f"g_sb{k}", [P, ST], F32)) for k in range(2)]
        hhf = [E(nc.sbuf_tensor(f"hhf{k}", [P, ST], F32)) for k in range(2)]
        x1s = [E(nc.sbuf_tensor(f"x1s{k}", [P, ST], F32)) for k in range(2)]
        res2 = E(nc.sbuf_tensor("res2", [P, DC * ST], F32))
        hh_hi = E(nc.sbuf_tensor("hh_hi", [P, HC * ST], BF16, side="right"))
        hh_lo = E(nc.sbuf_tensor("hh_lo", [P, HC * ST], BF16, side="right"))

        GB = [0, 1, 2]
        UB = [3, 4, 5]

        for m in range(NST):
            for hc in range(HC):
                sl = hc % 2
                ed = prep_fence if (m == 0 and hc < 2) else ()
                sc.step("sync", (lambda d, s: (lambda e: e.dma_start(out=d, in_=s)))(
                    wbufg[sl][:, :], wgT_s[hc, :, :]),
                    writes=[("wbufg", sl)], extra_deps=ed, sem=f"qg{sl}", n=16)
                sc.step("sync", (lambda d, s: (lambda e: e.dma_start(out=d, in_=s)))(
                    wbufu[sl][:, :], wuT_s[hc, :, :]),
                    writes=[("wbufu", sl)], extra_deps=ed, sem=f"qq{sl}", n=16)
                bg, bu_ = GB[hc % 3], UB[hc % 3]
                for (wb2, bb, tok) in [(wbufg[sl], bg, ("wbufg", sl)),
                                       (wbufu[sl], bu_, ("wbufu", sl))]:
                    k = 0
                    for dc in range(DC):
                        for (rhs, rt) in [
                            (x1hi[:, dc * TL + m * ST: dc * TL + (m + 1) * ST],
                             ("x1hi", dc, m)),
                            (x1lo[:, dc * TL + m * ST: dc * TL + (m + 1) * ST],
                             ("x1lo", dc, m)),
                        ]:
                            sc.step("tensor", (lambda o, l, r, st_, sp_: (
                                lambda e: e.matmul(o, l, r, start=st_, stop=sp_)))(
                                bank(bb), wb2[:, dc * P:(dc + 1) * P], rhs,
                                k == 0, k == 2 * DC - 1),
                                reads=[rt, tok], writes=[("pb", bb)],
                                lazy=(k < 2 * DC - 1))
                            k += 1
                sc.step("scalar", (lambda o, s: (lambda e: e.activation(
                    out=o, in_=s, func=AF.Silu, bias=0.0, scale=SCALE)))(
                    g_sb[sl][:, :], bank(bg)),
                    reads=[("pb", bg)], writes=[("g_sb", sl)])
                sc.step("vector", (lambda o, a, b2: (lambda e: e.tensor_tensor(
                    out=o, in0=a, in1=b2, op=OP.mult)))(
                    hhf[sl][:, :], g_sb[sl][:, :], bank(bu_)),
                    reads=[("g_sb", sl), ("pb", bu_)], writes=[("hhf", sl)])
                hi = hh_hi[:, hc * ST:(hc + 1) * ST]
                lo = hh_lo[:, hc * ST:(hc + 1) * ST]
                sc.step("scalar", (lambda o, s: (lambda e: e.mul(out=o, in_=s, mul=SCALE)))(
                    hi, hhf[sl][:, :]),
                    reads=[("hhf", sl)], writes=[("hhhi", hc)])
                sc.step("vector", (lambda o, s, h: (lambda e: e.scalar_tensor_tensor(
                    out=o, in0=s, scalar=SCALE, in1=h, op0=OP.mult, op1=OP.subtract)))(
                    lo, hhf[sl][:, :], hi),
                    reads=[("hhf", sl), ("hhhi", hc)], writes=[("hhlo", hc)])

            for hc in range(HC):
                sl = hc % 2
                sc.step("gpsimd", (lambda d, s: (lambda e: e.dma_start(out=d, in_=s)))(
                    wbufd[sl][:, :], wdT_s[hc, :, :]),
                    writes=[("wbufd", sl)],
                    extra_deps=prep_fence if (m == 0 and hc == 0) else (),
                    sem=f"qd{sl}", n=16)
                for dc in range(DC):
                    for pi, (rhs, rt) in enumerate([
                        (hh_hi[:, hc * ST:(hc + 1) * ST], ("hhhi", hc)),
                        (hh_lo[:, hc * ST:(hc + 1) * ST], ("hhlo", hc)),
                    ]):
                        k = hc * 2 + pi
                        sc.step("tensor", (lambda o, l, r, st_, sp_: (
                            lambda e: e.matmul(o, l, r, start=st_, stop=sp_)))(
                            bank(dc), wbufd[sl][:, dc * P:(dc + 1) * P], rhs,
                            k == 0, k == 2 * HC - 1),
                            reads=[rt, ("wbufd", sl)], writes=[("pb", dc)],
                            lazy=not (dc == DC - 1 and pi == 1))

            for dc in range(DC):
                sl = dc % 2
                uh = x1hi[:, dc * TL + m * ST: dc * TL + (m + 1) * ST]
                ul = x1lo[:, dc * TL + m * ST: dc * TL + (m + 1) * ST]
                sc.step("vector", (lambda o, a, b2: (lambda e: e.tensor_tensor(
                    out=o, in0=a, in1=b2, op=OP.add)))(x1s[sl][:, :], uh, ul),
                    reads=[("x1hi", dc, m), ("x1lo", dc, m)], writes=[("x1s", sl)])
                sc.step("vector", (lambda o, s, a: (lambda e: e.scalar_tensor_tensor(
                    out=o, in0=s, scalar=SCALE, in1=a, op0=OP.mult, op1=OP.add)))(
                    res2[:, dc * ST:(dc + 1) * ST], bank(dc), x1s[sl][:, :]),
                    reads=[("pb", dc), ("x1s", sl)], writes=[("res2", dc)])

            for g in range(4):
                gg = m * 4 + g
                sl = g % 2
                ln_group(lambda dc, g=g: res2[:, dc * ST + g * P: dc * ST + (g + 1) * P],
                         sl, lambda dc: [("res2", dc)])
                sc.step("gpsimd", (lambda d, s: (lambda e: e.dma_start(out=d, in_=s)))(
                    out_h[gg * P:(gg + 1) * P, :], x1tok[sl][:, :]),
                    reads=[("x1tokbuf", sl)], sem=f"qo{sl}", n=16)

        qo_tot = (sc.seq.get("qo0", 0), sc.seq.get("qo1", 0))

        @block.tensor
        def _(eng):
            sc.emit("tensor", eng, sems)

        @block.scalar
        def _(eng):
            sc.emit("scalar", eng, sems)

        @block.vector
        def _(eng):
            sc.emit("vector", eng, sems)

        @block.gpsimd
        def _(eng):
            sc.emit("gpsimd", eng, sems)
            eng.wait_ge(sems["qo0"], qo_tot[0])
            eng.wait_ge(sems["qo1"], qo_tot[1])

        @block.sync
        def _(eng):
            sc.emit("sync", eng, sems)
            eng.wait_ge(sems["qo0"], qo_tot[0])
            eng.wait_ge(sems["qo1"], qo_tot[1])

    return nc


_NC_CACHE = [None]


def kernel(L0_attn_u, L0_attn_Wl, L0_attn_bl, L0_attn_Wb, L0_attn_Wc, L0_attn_Dp,
           L0_ffn_w_gate, L0_ffn_w_up, L0_ffn_w_down):
    u = np.ascontiguousarray(np.asarray(L0_attn_u, dtype=np.float32))
    shared = dict(
        wl=np.ascontiguousarray(np.asarray(L0_attn_Wl, np.float32)),
        bl=np.ascontiguousarray(np.asarray(L0_attn_bl, np.float32)),
        wb=np.ascontiguousarray(np.asarray(L0_attn_Wb, np.float32)),
        wc=np.ascontiguousarray(np.asarray(L0_attn_Wc, np.float32)),
        dp=np.ascontiguousarray(np.asarray(L0_attn_Dp, np.float32)),
        wg=np.ascontiguousarray(np.asarray(L0_ffn_w_gate, np.int32)),
        wu=np.ascontiguousarray(np.asarray(L0_ffn_w_up, np.int32)),
        wd=np.ascontiguousarray(np.asarray(L0_ffn_w_down, np.int32)),
    )
    zeros = np.zeros((TL, D), np.float32)
    in_maps = []
    for c in range(8):
        b, h = divmod(c, 2)
        mmap = dict(shared)
        mmap["u_main"] = np.ascontiguousarray(u[b, h * TL:(h + 1) * TL])
        mmap["u_pre"] = np.ascontiguousarray(u[b, 0:TL]) if h == 1 else zeros
        in_maps.append(mmap)

    if _NC_CACHE[0] is None:
        _NC_CACHE[0] = build_nc()
    nc = _NC_CACHE[0]
    res = run_bass_kernel_spmd(nc, in_maps, core_ids=list(range(8)))

    out = np.empty((B, T, D), np.float32)
    for c in range(8):
        b, h = divmod(c, 2)
        out[b, h * TL:(h + 1) * TL] = res.results[c]["out"]
    return out
